# version 07
# baseline (speedup 1.0000x reference)
import math
import numpy as np
import ml_dtypes
import concourse.bass as bass
import concourse.mybir as mybir
from concourse.bass_utils import run_bass_kernel_spmd

F32 = mybir.dt.float32
BF16 = mybir.dt.bfloat16
U8 = mybir.dt.uint8
AF = mybir.ActivationFunctionType
OP = mybir.AluOpType
AX = mybir.AxisListType.X

NC = 8
D = 4096
T = 8192
TQ = 1024
H = 32
INT = 8192
LAT = 2048
NE = 8
LN_EPS = 1e-5
ISQ = 1.0 / math.sqrt(128.0)
DC = 32
NSLOT = 4
QW = 256

ARENA_B = 176 * 1024


def build_nc(stop_after=None, debug_dump=None):
    nc = bass.Bass("TRN2", target_bir_lowering=False)

    x = nc.dram_tensor("x", [T, D], F32, kind="ExternalInput")
    xq = nc.dram_tensor("xq", [TQ, D], F32, kind="ExternalInput")
    wq = nc.dram_tensor("wq", [D, D], F32, kind="ExternalInput")
    wk = nc.dram_tensor("wk", [D, D], F32, kind="ExternalInput")
    wv = nc.dram_tensor("wv", [D, D], F32, kind="ExternalInput")
    wo = nc.dram_tensor("wo", [D, D], F32, kind="ExternalInput")
    bq_fm = nc.dram_tensor("bq_fm", [128, DC], F32, kind="ExternalInput")
    bk_fm = nc.dram_tensor("bk_fm", [128, DC], F32, kind="ExternalInput")
    bo_fm = nc.dram_tensor("bo_fm", [128, DC], F32, kind="ExternalInput")
    g_fm = nc.dram_tensor("g_fm", [128, DC], F32, kind="ExternalInput")
    b_fm = nc.dram_tensor("b_fm", [128, DC], F32, kind="ExternalInput")
    bv_row = nc.dram_tensor("bv_row", [1, D], F32, kind="ExternalInput")
    gate_w = nc.dram_tensor("gate_w", [D, NE], F32, kind="ExternalInput")
    up_w = nc.dram_tensor("up_w", [D, INT], F32, kind="ExternalInput")
    down_w = nc.dram_tensor("down_w", [INT, D], F32, kind="ExternalInput")
    fc1_w = nc.dram_tensor("fc1_w", [D, LAT], F32, kind="ExternalInput")
    fc2_w = nc.dram_tensor("fc2_w", [LAT, D], F32, kind="ExternalInput")
    exp_w = nc.dram_tensor("exp_w", [LAT, LAT], F32, kind="ExternalInput")
    masks = nc.dram_tensor("masks", [NSLOT, 16, 128, QW], BF16, kind="ExternalInput")
    out_rows = nc.dram_tensor("out_rows", [TQ, D], F32, kind="ExternalOutput")

    ones_b = nc.inline_tensor(np.ones((128, 128), dtype=ml_dtypes.bfloat16), "ones_b")
    ones_f = nc.inline_tensor(np.ones((128, 128), dtype=np.float32), "ones_f")
    ident_f = nc.inline_tensor(np.eye(128, dtype=np.float32), "ident_f")
    eps_t = nc.inline_tensor(np.full((128, 1), LN_EPS, dtype=np.float32), "eps_t")

    x16 = nc.dram_tensor("x16", [T, D], BF16, kind="Internal")
    xq16 = nc.dram_tensor("xq16", [TQ, D], BF16, kind="Internal")
    wq16 = nc.dram_tensor("wq16", [D, D], BF16, kind="Internal")
    wk16 = nc.dram_tensor("wk16", [D, D], BF16, kind="Internal")
    wv16 = nc.dram_tensor("wv16", [D, D], BF16, kind="Internal")
    wo16 = nc.dram_tensor("wo16", [D, D], BF16, kind="Internal")
    kd = nc.dram_tensor("kd", [D, T], BF16, kind="Internal")
    vd = nc.dram_tensor("vd", [T, D], BF16, kind="Internal")
    att_d = nc.dram_tensor("att_d", [D, TQ], BF16, kind="Internal")
    x1d = nc.dram_tensor("x1d", [D, TQ], F32, kind="Internal")
    sd = nc.dram_tensor("sd", [D, TQ], F32, kind="Internal")
    up16 = nc.dram_tensor("up16", [D, INT], BF16, kind="Internal")
    down16 = nc.dram_tensor("down16", [INT, D], BF16, kind="Internal")
    fc1_16 = nc.dram_tensor("fc1_16", [D, LAT], BF16, kind="Internal")
    fc2_16 = nc.dram_tensor("fc2_16", [LAT, D], BF16, kind="Internal")
    exp16 = nc.dram_tensor("exp16", [LAT, LAT], BF16, kind="Internal")

    ENGS = ("sync", "gpsimd", "scalar", "vector", "tensor")
    Q = {e: [] for e in ENGS}
    C = {"ld": 0, "cst": 0, "st": 0, "pe": 0, "act": 0, "dve": 0}
    marks = {}

    def phase_mark(name):
        marks[name] = ({e: len(Q[e]) for e in ENGS}, dict(C))

    import contextlib
    ctx = contextlib.ExitStack()
    arena_t = ctx.enter_context(nc.sbuf_tensor("arena", [128, ARENA_B], U8))
    pA = ctx.enter_context(nc.psum_tensor("pA", [128, 1024], F32))
    pB = ctx.enter_context(nc.psum_tensor("pB", [128, 1024], F32))
    p4 = ctx.enter_context(nc.psum_tensor("p4", [128, 512], F32))
    p5 = ctx.enter_context(nc.psum_tensor("p5", [128, 512], F32))
    pC = ctx.enter_context(nc.psum_tensor("pC", [128, 1024], F32))
    p6 = pC[:, 0:512]
    p7 = pC[:, 512:1024]
    sem = {k: ctx.enter_context(nc.semaphore(k)) for k in C}
    block = ctx.enter_context(nc.Block())

    def em(eng, fn):
        Q[eng].append(fn)

    def view(off, dtype, cols, p0=0, parts=128):
        esz = mybir.dt.size(dtype)
        assert off % 64 == 0 and off + cols * esz <= ARENA_B, (off, cols, dtype)
        return arena_t[p0:p0 + parts, off:off + cols * esz].bitcast(dtype)

    bank_guard = {}

    def pe_wait_bank(tag):
        g = bank_guard.get(tag)
        if g is None:
            return None
        return g

    ones_b_sb = view(0, BF16, 128)
    ones_f_sb = view(320, F32, 128)
    ident_sb = view(832, F32, 128)
    bq_sb = view(1344, F32, DC)
    bk_sb = view(1472, F32, DC)
    bo_sb = view(1600, F32, DC)
    g_sb = view(1728, F32, DC)
    b_sb = view(1856, F32, DC)
    gate_sb = view(1984, BF16, DC * NE)
    eps_sb = view(2496, F32, 1)
    BASE = 4096

    def dma_ld(out_ap, in_ap, pre=None):
        C["ld"] += 16
        v = C["ld"]
        def f(e, o=out_ap, i=in_ap, pre=tuple(pre or ())):
            for s, vv in pre:
                e.wait_ge(sem[s], vv)
            e.dma_start(out=o, in_=i).then_inc(sem["ld"], 16)
        em("sync", f)
        return v

    def dma_ldx(out_ap, in_ap, pre=None):
        C["ld"] += 16
        v = C["ld"]
        def f(e, o=out_ap, i=in_ap, pre=tuple(pre or ())):
            for s, vv in pre:
                e.wait_ge(sem[s], vv)
            e.dma_start_transpose(o, i).then_inc(sem["ld"], 16)
        em("sync", f)
        return v

    def dma_cst(out_ap, in_ap, pre=None):
        C["cst"] += 16
        v = C["cst"]
        def f(e, o=out_ap, i=in_ap, pre=tuple(pre or ())):
            for s, vv in pre:
                e.wait_ge(sem[s], vv)
            e.dma_start(out=o, in_=i).then_inc(sem["cst"], 16)
        em("gpsimd", f)
        return v

    def dma_st(out_ap, in_ap, pre=None):
        C["st"] += 16
        v = C["st"]
        def f(e, o=out_ap, i=in_ap, pre=tuple(pre or ())):
            for s, vv in pre:
                e.wait_ge(sem[s], vv)
            e.dma_start(out=o, in_=i).then_inc(sem["st"], 16)
        em("scalar", f)
        return v

    for r in range(4):
        dma_cst(wk16[r * 1024:(r + 1) * 1024, :], wk[r * 1024:(r + 1) * 1024, :])
    cst_wk = C["cst"]
    cst_xchunk = []
    for r in range(8):
        dma_cst(x16[r * 1024:(r + 1) * 1024, :], x[r * 1024:(r + 1) * 1024, :])
        cst_xchunk.append(C["cst"])
    for r in range(4):
        dma_cst(wv16[r * 1024:(r + 1) * 1024, :], wv[r * 1024:(r + 1) * 1024, :])
    cst_wkv = C["cst"]
    cst_x = cst_xchunk[-1]
    dma_cst(xq16[:, :], xq[:, :])
    cst_xq = C["cst"]
    for r in range(4):
        dma_cst(wq16[r * 1024:(r + 1) * 1024, :], wq[r * 1024:(r + 1) * 1024, :])
    cst_wq = C["cst"]
    for r in range(4):
        dma_cst(wo16[r * 1024:(r + 1) * 1024, :], wo[r * 1024:(r + 1) * 1024, :])
    cst_wo = C["cst"]
    dma_cst(gate_sb.rearrange("p (kc e) -> p kc e", e=NE),
            gate_w.ap().rearrange("(kc p) e -> p kc e", p=128))
    cst_gate = C["cst"]
    for r in range(8):
        dma_cst(up16[r * 512:(r + 1) * 512, :], up_w[r * 512:(r + 1) * 512, :])
    for r in range(8):
        dma_cst(down16[r * 1024:(r + 1) * 1024, :], down_w[r * 1024:(r + 1) * 1024, :])
    for r in range(2):
        dma_cst(fc1_16[r * 2048:(r + 1) * 2048, :], fc1_w[r * 2048:(r + 1) * 2048, :])
        dma_cst(fc2_16[r * 1024:(r + 1) * 1024, :], fc2_w[r * 1024:(r + 1) * 1024, :])
    dma_cst(exp16[:, :], exp_w[:, :])
    cst_ffn = C["cst"]

    dma_ld(ones_b_sb, ones_b[:, :])
    dma_ld(ones_f_sb, ones_f[:, :])
    dma_ld(ident_sb, ident_f[:, :])
    dma_ld(bq_sb, bq_fm[:, :])
    dma_ld(bk_sb, bk_fm[:, :])
    dma_ld(bo_sb, bo_fm[:, :])
    dma_ld(g_sb, g_fm[:, :])
    dma_ld(b_sb, b_fm[:, :])
    dma_ld(eps_sb, eps_t[:, :])
    ld_consts = C["ld"]
    phase_mark("p0")

    slabA_o, slabB_o, xfm_o = BASE, BASE + 65536, BASE + 131072
    bvb_o = xfm_o + 32768
    rrow_o = bvb_o + 8192
    ering_o = rrow_o + 2048
    slabs = [view(slabA_o, BF16, DC * 1024), view(slabB_o, BF16, DC * 1024)]
    xfm = view(xfm_o, BF16, DC * 512)
    bvb = view(bvb_o, F32, 2048)
    Rrow = view(rrow_o, F32, 512)
    ering = [view(ering_o + i * 1024, BF16, 512) for i in range(2)]
    er_guard = [None] * 2
    er_i = 0

    pairs = [("k", 0, 1), ("k", 2, 3), ("v", 0, 1), ("v", 2, 3)]
    slab_pe_done = 0
    xfm_pe_done = 0

    for pi, (mat, ibA, ibB) in enumerate(pairs):
        w16t = wk16 if mat == "k" else wv16
        pre = []
        if pi == 0:
            pre = [("cst", cst_wk)]
        elif pi == 2:
            pre = [("cst", cst_wkv)]
        pre = pre + ([("pe", slab_pe_done)] if slab_pe_done else [])
        slab_lds = []
        for si, ib in ((0, ibA), (1, ibB)):
            for dc in range(DC):
                slab_lds.append(dma_ldx(
                    slabs[si][:, dc * 1024:(dc + 1) * 1024],
                    w16t[ib * 1024:(ib + 1) * 1024, dc * 128:(dc + 1) * 128],
                    pre=pre if (si == 0 and dc == 0) else None))
        slab_ready = slab_lds[-1]

        if mat == "v":
            rrow_pe_guard = 0
            for half in range(4):
                i0 = ibA * 1024 + half * 512
                C["dve"] += 1
                dz = C["dve"]
                def fz(e, g=rrow_pe_guard):
                    if g:
                        e.wait_ge(sem["pe"], g)
                    e.memset(Rrow, 0.0).then_inc(sem["dve"], 1)
                em("vector", fz)
                lv = dma_ld(Rrow[0:1, 0:512], bv_row[0:1, i0:i0 + 512],
                            pre=[("dve", dz)])
                C["pe"] += 1
                pv = C["pe"]
                gp6 = bank_guard.get("p6")
                def fb(e, lv=lv, gp6=gp6):
                    e.wait_ge(sem["ld"], lv)
                    if gp6 is not None:
                        e.wait_ge(sem[gp6[0]], gp6[1])
                    e.matmul(p6, ones_f_sb, Rrow, start=True,
                             stop=True).then_inc(sem["pe"], 1)
                em("tensor", fb)
                rrow_pe_guard = pv
                C["dve"] += 1
                dv = C["dve"]
                def fc(e, pv=pv, half=half):
                    e.wait_ge(sem["pe"], pv)
                    e.tensor_copy(bvb[:, half * 512:(half + 1) * 512],
                                  p6).then_inc(sem["dve"], 1)
                em("vector", fc)
                bank_guard["p6"] = ("dve", dv)

        for tt in range(16):
            xpre = [("cst", cst_xchunk[tt // 2])] if pi == 0 else []
            if xfm_pe_done:
                xpre.append(("pe", xfm_pe_done))
            xlds = []
            for dc in range(DC):
                xlds.append(dma_ldx(
                    xfm[:, dc * 512:(dc + 1) * 512],
                    x16[tt * 512:(tt + 1) * 512, dc * 128:(dc + 1) * 128],
                    pre=xpre if dc == 0 else None))
            x_ready = xlds[-1]

            if mat == "k":
                for si, ib in ((0, ibA), (1, ibB)):
                    for ic in range(8):
                        ig = ib * 8 + ic
                        tag = "p4" if (ic % 2 == 0) else "p5"
                        pp = p4 if (ic % 2 == 0) else p5
                        gd = bank_guard.get(tag)
                        C["pe"] += 1
                        pv = C["pe"]
                        def fK(e, si=si, ic=ic, pp=pp, x_ready=x_ready,
                               slab_ready=slab_ready, gd=gd):
                            e.wait_ge(sem["ld"], max(x_ready, slab_ready))
                            if gd is not None:
                                e.wait_ge(sem[gd[0]], gd[1])
                            for dc in range(DC):
                                ins = e.matmul(
                                    pp[:, :],
                                    slabs[si][:, dc * 1024 + ic * 128: dc * 1024 + ic * 128 + 128],
                                    xfm[:, dc * 512:(dc + 1) * 512],
                                    start=(dc == 0), stop=(dc == DC - 1))
                            ins.then_inc(sem["pe"], 1)
                        em("tensor", fK)
                        buf, guard = ering[er_i % 2], er_guard[er_i % 2]
                        C["act"] += 1
                        av = C["act"]
                        def gK(e, pv=pv, pp=pp, buf=buf, ig=ig, guard=guard):
                            e.wait_ge(sem["pe"], pv)
                            if guard is not None:
                                e.wait_ge(sem["st"], guard)
                            e.activation(buf, pp[:, :], AF.Identity,
                                         bias=bk_sb[:, ig:ig + 1],
                                         scale=1.0).then_inc(sem["act"], 1)
                        em("scalar", gK)
                        bank_guard[tag] = ("act", av)
                        er_guard[er_i % 2] = dma_st(
                            kd[ig * 128:(ig + 1) * 128, tt * 512:(tt + 1) * 512],
                            buf, pre=[("act", av)])
                        er_i += 1
            else:
                for si, ib in ((0, ibA), (1, ibB)):
                    for nb in range(2):
                        for t1 in range(4):
                            k2 = (nb * 4 + t1) % 2
                            tag = "p4" if k2 == 0 else "p5"
                            pp = p4 if k2 == 0 else p5
                            gd = bank_guard.get(tag)
                            C["pe"] += 1
                            pv = C["pe"]
                            def fV(e, si=si, nb=nb, t1=t1, pp=pp, gd=gd,
                                   x_ready=x_ready, slab_ready=slab_ready):
                                e.wait_ge(sem["ld"], max(x_ready, slab_ready))
                                if gd is not None:
                                    e.wait_ge(sem[gd[0]], gd[1])
                                for dc in range(DC):
                                    ins = e.matmul(
                                        pp[:, :],
                                        xfm[:, dc * 512 + t1 * 128: dc * 512 + t1 * 128 + 128],
                                        slabs[si][:, dc * 1024 + nb * 512: dc * 1024 + nb * 512 + 512],
                                        start=(dc == 0), stop=(dc == DC - 1))
                                ins.then_inc(sem["pe"], 1)
                            em("tensor", fV)
                            buf, guard = ering[er_i % 2], er_guard[er_i % 2]
                            C["dve"] += 1
                            dv = C["dve"]
                            boff = (ib - ibA) * 1024 + nb * 512
                            def gV(e, pv=pv, pp=pp, buf=buf, boff=boff, guard=guard):
                                e.wait_ge(sem["pe"], pv)
                                if guard is not None:
                                    e.wait_ge(sem["st"], guard)
                                e.tensor_add(buf, pp[:, :],
                                             bvb[:, boff:boff + 512]).then_inc(sem["dve"], 1)
                            em("vector", gV)
                            bank_guard[tag] = ("dve", dv)
                            tok0 = tt * 512 + t1 * 128
                            col0 = ib * 1024 + nb * 512
                            er_guard[er_i % 2] = dma_st(
                                vd[tok0:tok0 + 128, col0:col0 + 512], buf,
                                pre=[("dve", dv)])
                            er_i += 1
            xfm_pe_done = C["pe"]
            if tt == 0:
                phase_mark(f"{mat}{pi}tt0")
        slab_pe_done = C["pe"]
        phase_mark(f"{mat}{pi}")
    st_kv_done = C["st"]
    phase_mark("p1")

    qsb_o = BASE
    qslab_o = BASE + 65536
    xqfm_o = BASE + 131072
    Q_sb = view(qsb_o, BF16, DC * 1024)
    qslab = view(qslab_o, BF16, DC * 1024)
    xqfm = view(xqfm_o, BF16, DC * 512)
    p1_pe_done = C["pe"]
    p1_act_done = C["act"]
    p1_dve_done = C["dve"]

    qslab_pe_done = 0
    xq_pe_done = 0
    for ib in range(4):
        pre = [("cst", cst_wq)] if ib == 0 else []
        pre.append(("pe", qslab_pe_done if qslab_pe_done else p1_pe_done))
        sl = []
        for dc in range(DC):
            sl.append(dma_ldx(
                qslab[:, dc * 1024:(dc + 1) * 1024],
                wq16[ib * 1024:(ib + 1) * 1024, dc * 128:(dc + 1) * 128],
                pre=pre if dc == 0 else None))
        slab_ready = sl[-1]
        for qh in range(2):
            xpre = [("cst", cst_xq)] if (ib == 0 and qh == 0) else []
            xpre.append(("pe", xq_pe_done if xq_pe_done else p1_pe_done))
            xl = []
            for dc in range(DC):
                xl.append(dma_ldx(
                    xqfm[:, dc * 512:(dc + 1) * 512],
                    xq16[qh * 512:(qh + 1) * 512, dc * 128:(dc + 1) * 128],
                    pre=xpre if dc == 0 else None))
            x_ready = xl[-1]
            for ic in range(8):
                ig = ib * 8 + ic
                tag = "p4" if (ic % 2 == 0) else "p5"
                pp = p4 if (ic % 2 == 0) else p5
                gd = bank_guard.get(tag)
                C["pe"] += 1
                pv = C["pe"]
                def fQ(e, ic=ic, pp=pp, gd=gd, x_ready=x_ready, slab_ready=slab_ready):
                    e.wait_ge(sem["ld"], max(x_ready, slab_ready))
                    if gd is not None:
                        e.wait_ge(sem[gd[0]], gd[1])
                    for dc in range(DC):
                        ins = e.matmul(
                            pp[:, :],
                            qslab[:, dc * 1024 + ic * 128: dc * 1024 + ic * 128 + 128],
                            xqfm[:, dc * 512:(dc + 1) * 512],
                            start=(dc == 0), stop=(dc == DC - 1))
                    ins.then_inc(sem["pe"], 1)
                em("tensor", fQ)
                C["act"] += 1
                av = C["act"]
                def gQ(e, pv=pv, pp=pp, ig=ig, qh=qh):
                    e.wait_ge(sem["pe"], pv)
                    e.activation(
                        Q_sb[:, ig * 1024 + qh * 512: ig * 1024 + qh * 512 + 512],
                        pp[:, :], AF.Identity, bias=bq_sb[:, ig:ig + 1],
                        scale=1.0).then_inc(sem["act"], 1)
                em("scalar", gQ)
                bank_guard[tag] = ("act", av)
            xq_pe_done = C["pe"]
        qslab_pe_done = C["pe"]
    bank_guard["p45"] = ("act", C["act"])
    phase_mark("p2")

    mask_o = qslab_o
    kh_o = mask_o + 32768
    vh_o = xqfm_o
    pt_o = bvb_o
    rc_o = pt_o + 4096
    attb_o = rc_o + 1024 + 1024
    mask_sb = view(mask_o, BF16, NSLOT * 16 * QW)
    kh = [view(kh_o + i * 16384, BF16, T) for i in range(2)]
    vh = [view(vh_o + i * 16384, BF16, T) for i in range(2)]
    pt = [view(pt_o, BF16, 1024), view(pt_o + 2048, BF16, 1024),
          view(176128, BF16, 1024)]
    rc_sb = view(rc_o, F32, QW)
    attb = [view(attb_o + i * 512, BF16, QW) for i in range(4)]

    mld = []
    for j in range(NSLOT):
        mld.append(dma_ld(
            mask_sb[:, j * 16 * QW:(j + 1) * 16 * QW].rearrange(
                "p (m q) -> p m q", q=QW),
            masks[j, :, :, :].rearrange("m p q -> p m q"),
            pre=[("pe", qslab_pe_done)] if j == 0 else None))
    ld_mask = mld[-1]

    def mask_view(j, m):
        o = (j * 16 + m) * QW
        return mask_sb[:, o:o + QW]

    kv_ready = [0, 0]
    kv_guard = [0, 0]
    att_guard = [None] * 4
    pt_guard = [None, None, None]
    g_idx = 0
    for h in range(H):
        hb = h % 2
        pre = [("st", st_kv_done), ("pe", xq_pe_done)] if h == 0 else []
        if kv_guard[hb]:
            pre.append(("pe", kv_guard[hb]))
        dma_ld(kh[hb], kd[h * 128:(h + 1) * 128, :], pre=pre)
        kv_ready[hb] = dma_ld(
            vh[hb].rearrange("p (kb n) -> p kb n", n=128),
            vd.ap().rearrange("(kb p) i -> p kb i", p=128)[:, :, h * 128:(h + 1) * 128])

        for j in range(NSLOT):
            ngrp = (j + 1) * 4
            for g in range(ngrp):
                sb_i = g_idx % 3
                ppS = (pA, pB, pC)[sb_i]
                tagS = ("pA", "pB", "pC")[sb_i]
                ptb = pt[sb_i]
                gd = bank_guard.get(tagS)
                C["pe"] += 1
                pv = C["pe"]
                def fS(e, h=h, j=j, g=g, ppS=ppS, hb=hb, gd=gd,
                       kvr=kv_ready[hb], lm=ld_mask):
                    e.wait_ge(sem["ld"], max(kvr, lm))
                    if gd is not None:
                        e.wait_ge(sem[gd[0]], gd[1])
                    for u in range(4):
                        kb = g * 4 + u
                        ins = e.matmul(
                            ppS[:, u * 256:(u + 1) * 256],
                            kh[hb][:, kb * 128:(kb + 1) * 128],
                            Q_sb[:, h * 1024 + j * 256: h * 1024 + j * 256 + 256],
                            start=True, stop=True)
                    ins.then_inc(sem["pe"], 1)
                em("tensor", fS)
                ptg = pt_guard[sb_i]
                C["act"] += 1
                av = C["act"]
                def fE(e, pv=pv, ppS=ppS, ptb=ptb, ptg=ptg):
                    e.wait_ge(sem["pe"], pv)
                    if ptg is not None:
                        e.wait_ge(sem[ptg[0]], ptg[1])
                    e.activation(ptb, ppS[:, :], AF.Exp, bias=0.0,
                                 scale=ISQ).then_inc(sem["act"], 1)
                em("scalar", fE)
                bank_guard[tagS] = ("act", av)
                dve0 = C["dve"]
                for u in range(4):
                    m = g * 4 + u - j * 16
                    if 0 <= m < 16:
                        C["dve"] += 1
                        def fM(e, av=av, ptb=ptb, u=u, j=j, m=m):
                            e.wait_ge(sem["act"], av)
                            e.tensor_mul(ptb[:, u * 256:(u + 1) * 256],
                                         ptb[:, u * 256:(u + 1) * 256],
                                         mask_view(j, m)).then_inc(sem["dve"], 1)
                        em("vector", fM)
                masked = C["dve"] > dve0
                gp45 = bank_guard.get("p45")
                C["pe"] += 1
                pav = C["pe"]
                def fA(e, g=g, ptb=ptb, hb=hb, av=av, dv=C["dve"], masked=masked,
                       first=(g == 0), last=(g == ngrp - 1), gp45=gp45):
                    if masked:
                        e.wait_ge(sem["dve"], dv)
                    else:
                        e.wait_ge(sem["act"], av)
                    if first and gp45 is not None:
                        e.wait_ge(sem[gp45[0]], gp45[1])
                    for u in range(4):
                        kb = g * 4 + u
                        e.matmul(p4[:, 0:256], vh[hb][:, kb * 128:(kb + 1) * 128],
                                 ptb[:, u * 256:(u + 1) * 256],
                                 start=(first and u == 0), stop=(last and u == 3))
                    for u in range(4):
                        kb = g * 4 + u
                        ins = e.matmul(p5[:, 0:256], ones_b_sb,
                                       ptb[:, u * 256:(u + 1) * 256],
                                       start=(first and u == 0), stop=(last and u == 3))
                    ins.then_inc(sem["pe"], 1)
                em("tensor", fA)
                pt_guard[sb_i] = ("pe", pav)
                g_idx += 1
            ab = attb[(h * NSLOT + j) % 4]
            ag = att_guard[(h * NSLOT + j) % 4]
            C["dve"] += 2
            dv = C["dve"]
            def fT(e, pv=C["pe"], ab=ab, ag=ag, dv=dv):
                e.wait_ge(sem["pe"], pv)
                if ag is not None:
                    e.wait_ge(sem["st"], ag)
                e.reciprocal(rc_sb, p5[:, 0:256]).then_inc(sem["dve"], 1)
                e.wait_ge(sem["dve"], dv - 1)
                e.tensor_mul(ab, p4[:, 0:256], rc_sb).then_inc(sem["dve"], 1)
            em("vector", fT)
            bank_guard["p45"] = ("dve", dv)
            att_guard[(h * NSLOT + j) % 4] = dma_st(
                att_d[h * 128:(h + 1) * 128, j * 256:(j + 1) * 256], ab,
                pre=[("dve", dv)])
        kv_guard[hb] = C["pe"]
    p3_pe_done = C["pe"]
    st_att_done = C["st"]
    phase_mark("p3")

    RA, RB, RC, RD = BASE, BASE + 32768, BASE + 98304, BASE + 163840
    att_t = view(RA, BF16, DC * 512)
    x1b = view(RA, BF16, DC * 512)
    woslab = view(RB, BF16, DC * 1024)
    lat_sb = view(RB, BF16, 16 * 512)
    relu_t = [view(RB + 16384 + i * 2048, F32, 512) for i in range(2)]
    moe_sb = view(RB + 16384, BF16, 16 * 512)
    upslab = [view(RB + 32768 + i * 16384, BF16, DC * 256) for i in range(2)]
    dwslab = [view(RB + 32768, BF16, 64 * 128), view(RB + 49152, BF16, 64 * 128)]
    r1 = view(RC, F32, DC * 512)
    h_sb = view(RC, BF16, 64 * 512)
    r2 = view(RC, F32, DC * 512)
    sq = [view(RD + i * 2048, F32, 512) for i in range(2)]
    mu_sb = view(RD + 4096, F32, 512)
    rstd_sb = view(RD + 6144, F32, 512)
    var_sb = view(RD + 8192, F32, 512)
    xqtm = ([view(RD + 10240 + i * 512, F32, 128) for i in range(4)] +
            [view(174080 + i * 512, F32, 128) for i in range(4)])
    lg_sb = view(RD + 4096, F32, 4 * NE)
    t2_sb = view(RD + 4096 + 256, F32, 4)
    scr0 = view(RD + 4096 + 320, F32, 8)
    scr1 = view(RD + 4096 + 384, F32, 8)
    Rr = view(RD + 6144, F32, 512)
    t2b = view(RD + 8192, F32, 512)
    mu2_sb = view(RD + 6144, F32, 512)
    rstd2_sb = view(RD + 10240, F32, 512)
    mu3_sb = view(RD + 6144, F32, 512)
    rstd3_sb = view(RD + 10240, F32, 512)
    Sring = [view(RA + i * 2048, F32, 512) for i in range(4)]
    f2slab = [view(RA + 8192 + i * 8192, BF16, 16 * 256) for i in range(2)]
    Sld = [view(RA + 16384 + i * 2048, F32, 512) for i in range(2)]
    x1ld = [view(RA + 20480 + i * 2048, F32, 512) for i in range(2)]
    osb = view(RA + 8192, F32, D)
    Sld = [view(RA + 24576 + i * 2048, F32, 512) for i in range(2)]
    x1ld = [view(RA + 28672 + i * 2048, F32, 512) for i in range(2)]

    def fin_stats(mu_dst, rstd_dst, wait_pe):
        base = C["dve"]
        C["dve"] += 4
        dv1 = C["dve"]
        def f1(e, wait_pe=wait_pe, mu_dst=mu_dst, rstd_dst=rstd_dst, base=base):
            e.wait_ge(sem["pe"], wait_pe)
            e.tensor_scalar(mu_dst, p6, 1.0 / D, None, OP.mult).then_inc(sem["dve"], 1)
            e.tensor_scalar(var_sb, p7, 1.0 / D, None, OP.mult).then_inc(sem["dve"], 1)
            e.wait_ge(sem["dve"], base + 2)
            e.tensor_mul(rstd_dst, mu_dst, mu_dst).then_inc(sem["dve"], 1)
            e.wait_ge(sem["dve"], base + 3)
            e.tensor_sub(var_sb, var_sb, rstd_dst).then_inc(sem["dve"], 1)
        em("vector", f1)
        C["act"] += 1
        av = C["act"]
        def f2(e, dv1=dv1, rstd_dst=rstd_dst):
            e.wait_ge(sem["dve"], dv1)
            e.activation(rstd_dst, var_sb, AF.Sqrt, bias=eps_sb,
                         scale=1.0).then_inc(sem["act"], 1)
        em("scalar", f2)
        C["dve"] += 1
        dv2 = C["dve"]
        def f3(e, av=av, rstd_dst=rstd_dst):
            e.wait_ge(sem["act"], av)
            e.reciprocal(rstd_dst, rstd_dst).then_inc(sem["dve"], 1)
        em("vector", f3)
        bank_guard["p6"] = ("dve", dv1)
        bank_guard["p7"] = ("dve", dv1)
        return dv2

    s_guard = [None] * 4
    st_x1_done = {}
    st_s_done = {}
    slab_guard = {}
    ln2_dve_guard = {}
    os_guard = [None]
    p45_free_for_p4 = ("dve", C["dve"])
    for tile in range(2):
        slab_guard.clear()
        t0 = tile * 512
        atv = []
        att_pre = [("st", st_att_done), ("pe", p3_pe_done)] if tile == 0 else \
                  [("pe", C["pe"])]
        for ac in range(DC):
            atv.append(dma_ld(att_t[:, ac * 512:(ac + 1) * 512],
                              att_d[ac * 128:(ac + 1) * 128, t0:t0 + 512],
                              pre=att_pre if ac == 0 else None))
        att_ready = atv[-1]
        woslab_pe_done = 0
        for jb in range(4):
            pre = [("cst", cst_wo)] if (tile == 0 and jb == 0) else []
            pre.append(("pe", woslab_pe_done if woslab_pe_done else C["pe"]))
            sl = []
            for ac in range(DC):
                sl.append(dma_ldx(
                    woslab[:, ac * 1024:(ac + 1) * 1024],
                    wo16[jb * 1024:(jb + 1) * 1024, ac * 128:(ac + 1) * 128],
                    pre=pre if ac == 0 else None))
            slab_ready = sl[-1]
            for jc8 in range(8):
                jg = jb * 8 + jc8
                tag = "p4" if (jc8 % 2 == 0) else "p5"
                pp = p4 if (jc8 % 2 == 0) else p5
                xv = []
                xbase = (jg % 2) * 4
                for t1 in range(4):
                    xv.append(dma_ld(
                        xqtm[xbase + t1],
                        xq[t0 + t1 * 128: t0 + (t1 + 1) * 128, jg * 128:(jg + 1) * 128],
                        pre=[("pe", C["pe"])] if t1 == 0 else None))
                gd = bank_guard.get(tag) if (tile == 0 and jb == 0 and jc8 < 2) \
                    else bank_guard.get(tag)
                if tile == 0 and jb == 0 and jc8 < 2:
                    gd = p45_free_for_p4
                C["pe"] += 1
                pv = C["pe"]
                def fO(e, jc8=jc8, pp=pp, att_ready=att_ready, gd=gd,
                       slab_ready=slab_ready, xv=tuple(xv), xb=(jg % 2) * 4):
                    e.wait_ge(sem["ld"], max(att_ready, slab_ready))
                    if gd is not None:
                        e.wait_ge(sem[gd[0]], gd[1])
                    for ac in range(DC):
                        e.matmul(pp[:, :],
                                 woslab[:, ac * 1024 + jc8 * 128: ac * 1024 + jc8 * 128 + 128],
                                 att_t[:, ac * 512:(ac + 1) * 512],
                                 start=(ac == 0), stop=False)
                    for t1 in range(4):
                        e.wait_ge(sem["ld"], xv[t1])
                        ins = e.matmul(pp[:, t1 * 128:(t1 + 1) * 128],
                                       xqtm[xb + t1], ident_sb,
                                       is_transpose=True, start=False,
                                       stop=(t1 == 3))
                    ins.then_inc(sem["pe"], 1)
                em("tensor", fO)
                C["act"] += 1
                av = C["act"]
                def gO(e, pv=pv, pp=pp, jg=jg):
                    e.wait_ge(sem["pe"], pv)
                    e.activation(r1[:, jg * 512:(jg + 1) * 512], pp[:, :],
                                 AF.Identity, bias=bo_sb[:, jg:jg + 1],
                                 scale=1.0).then_inc(sem["act"], 1)
                em("scalar", gO)
                bank_guard[tag] = ("act", av)
            woslab_pe_done = C["pe"]
        act_op_done = C["act"]

        gp6 = bank_guard.get("p6") or bank_guard.get("pC")
        gp7 = bank_guard.get("p7") or bank_guard.get("pC")
        if tile == 0:
            gp6 = bank_guard.get("pC") or gp6
            gp7 = bank_guard.get("pC") or gp7
        for jg in range(DC):
            C["act"] += 1
            av = C["act"]
            def gSq(e, jg=jg, wa=act_op_done, buf=sq[jg % 2],
                    wpe=(C["pe"] if jg < 2 else C["pe"] - 1)):
                e.wait_ge(sem["act"], wa)
                e.wait_ge(sem["pe"], wpe)
                e.activation(buf, r1[:, jg * 512:(jg + 1) * 512],
                             AF.Square).then_inc(sem["act"], 1)
            em("scalar", gSq)
            C["pe"] += 1
            def fSt(e, jg=jg, av=av, buf=sq[jg % 2], gp6=gp6, gp7=gp7,
                    first=(jg == 0), last=(jg == DC - 1)):
                e.wait_ge(sem["act"], av)
                if first and gp6 is not None:
                    e.wait_ge(sem[gp6[0]], gp6[1])
                if first and gp7 is not None:
                    e.wait_ge(sem[gp7[0]], gp7[1])
                e.matmul(p6, ones_f_sb, r1[:, jg * 512:(jg + 1) * 512],
                         start=first, stop=last)
                ins = e.matmul(p7, ones_f_sb, buf, start=first, stop=last)
                ins.then_inc(sem["pe"], 1)
            em("tensor", fSt)
        dv_st1 = fin_stats(mu_sb, rstd_sb, C["pe"])

        for jg in range(DC):
            C["dve"] += 2
            dv = C["dve"]
            def fAp(e, jg=jg, dvs=dv_st1, dv=dv):
                e.wait_ge(sem["dve"], dvs)
                dst = r1[:, jg * 512:(jg + 1) * 512]
                e.tensor_sub(dst, dst, mu_sb).then_inc(sem["dve"], 1)
                e.wait_ge(sem["dve"], dv - 1)
                e.tensor_mul(dst, dst, rstd_sb).then_inc(sem["dve"], 1)
            em("vector", fAp)
            C["act"] += 1
            av = C["act"]
            def fCa(e, jg=jg, dv=dv):
                e.wait_ge(sem["dve"], dv)
                e.activation(x1b[:, jg * 512:(jg + 1) * 512],
                             r1[:, jg * 512:(jg + 1) * 512],
                             AF.Copy).then_inc(sem["act"], 1)
            em("scalar", fCa)
            dma_st(x1d[jg * 128:(jg + 1) * 128, t0:t0 + 512],
                   r1[:, jg * 512:(jg + 1) * 512], pre=[("act", av)])
        st_x1_done[tile] = C["st"]
        act_x1b_done = C["act"]
        phase_mark(f"ln1_{tile}")

        for t1 in range(4):
            gd = bank_guard.get("p6")
            C["pe"] += 1
            pv = C["pe"]
            def fG(e, t1=t1, wa=act_x1b_done, gd=gd, cg=cst_gate):
                e.wait_ge(sem["cst"], cg)
                e.wait_ge(sem["act"], wa)
                if gd is not None:
                    e.wait_ge(sem[gd[0]], gd[1])
                for dc in range(DC):
                    ins = e.matmul(
                        pC[:, 0:NE],
                        x1b[:, dc * 512 + t1 * 128: dc * 512 + t1 * 128 + 128],
                        gate_sb[:, dc * NE:(dc + 1) * NE],
                        start=(dc == 0), stop=(dc == DC - 1))
                ins.then_inc(sem["pe"], 1)
            em("tensor", fG)
            base = C["dve"]
            C["dve"] += 22
            dvt = C["dve"]
            def fL(e, pv=pv, t1=t1, base=base):
                e.wait_ge(sem["pe"], pv)
                lg = lg_sb[:, t1 * NE:(t1 + 1) * NE]
                n = base
                e.tensor_copy(lg, pC[:, 0:NE]).then_inc(sem["dve"], 1); n += 1
                e.wait_ge(sem["dve"], n)
                first = True
                for d in range(1, NE):
                    w_ = NE - d
                    e.tensor_add(scr0[:, 0:w_], lg[:, 0:w_], lg[:, d:d + w_]).then_inc(sem["dve"], 1); n += 1
                    e.wait_ge(sem["dve"], n)
                    e.reduce_max(scr1[:, 0:1], scr0[:, 0:w_], AX).then_inc(sem["dve"], 1); n += 1
                    e.wait_ge(sem["dve"], n)
                    if first:
                        e.tensor_copy(t2_sb[:, t1:t1 + 1], scr1[:, 0:1]).then_inc(sem["dve"], 1); n += 1
                        first = False
                    else:
                        e.tensor_max(t2_sb[:, t1:t1 + 1], t2_sb[:, t1:t1 + 1],
                                     scr1[:, 0:1]).then_inc(sem["dve"], 1); n += 1
                    e.wait_ge(sem["dve"], n)
            em("vector", fL)
            bank_guard["p6"] = ("dve", dvt)
            gd7 = bank_guard.get("p7")
            C["pe"] += 1
            pvt = C["pe"]
            def fTr(e, dvt=dvt, t1=t1, gd7=gd7):
                e.wait_ge(sem["dve"], dvt)
                if t1 == 0 and gd7 is not None:
                    e.wait_ge(sem[gd7[0]], gd7[1])
                e.matmul(pC[0:1, 512:640], t2_sb[:, t1:t1 + 1], ident_sb,
                         is_transpose=True, start=True,
                         stop=True).then_inc(sem["pe"], 1)
            em("tensor", fTr)
            C["dve"] += (2 if t1 == 0 else 1)
            dvr = C["dve"] - 1 if t1 == 0 else C["dve"]
            def fRr(e, pvt=pvt, t1=t1, dv=dvr):
                e.wait_ge(sem["pe"], pvt)
                if t1 == 0:
                    e.memset(Rr, 0.0).then_inc(sem["dve"], 1)
                    e.wait_ge(sem["dve"], dv)
                e.tensor_copy(Rr[0:1, t1 * 128:(t1 + 1) * 128],
                              pC[0:1, 512:640]).then_inc(sem["dve"], 1)
            em("vector", fRr)
            bank_guard["p7"] = ("dve", C["dve"])
        C["pe"] += 1
        pvb = C["pe"]
        def fB(e, wd=C["dve"], gd=bank_guard.get("p7")):
            e.wait_ge(sem["dve"], wd)
            e.matmul(p7, ones_f_sb, Rr, start=True,
                     stop=True).then_inc(sem["pe"], 1)
        em("tensor", fB)
        C["dve"] += 1
        dv_t2b = C["dve"]
        def fB2(e, pvb=pvb):
            e.wait_ge(sem["pe"], pvb)
            e.tensor_copy(t2b, p7).then_inc(sem["dve"], 1)
        em("vector", fB2)
        bank_guard["p7"] = ("dve", dv_t2b)

        for mp in range(32):
            mb = mp % 2
            g = slab_guard.get(("up", mb)) if mp >= 2 else C["pe"]
            upr = dma_ld(
                upslab[mb].rearrange("p (kc n) -> p kc n", n=256),
                up16.ap().rearrange("(kc p) n -> p kc n", p=128)[:, :, mp * 256:(mp + 1) * 256],
                pre=[("pe", g)] + ([("cst", cst_ffn)] if (tile == 0 and mp == 0) else []))
            for s in range(2):
                m = 2 * mp + s
                tag = "p4" if m % 2 == 0 else "p5"
                pp = p4 if m % 2 == 0 else p5
                gd = bank_guard.get(tag)
                C["pe"] += 1
                pv = C["pe"]
                def fU(e, mb=mb, s=s, pp=pp, upr=upr, gd=gd, m=m,
                       wst=st_x1_done[tile]):
                    e.wait_ge(sem["ld"], upr)
                    if m == 0:
                        e.wait_ge(sem["st"], wst)
                    if gd is not None:
                        e.wait_ge(sem[gd[0]], gd[1])
                    for dc in range(DC):
                        ins = e.matmul(pp[:, :],
                                       upslab[mb][:, dc * 256 + s * 128: dc * 256 + s * 128 + 128],
                                       x1b[:, dc * 512:(dc + 1) * 512],
                                       start=(dc == 0), stop=(dc == DC - 1))
                    ins.then_inc(sem["pe"], 1)
                em("tensor", fU)
                slab_guard[("up", mb)] = pv
                C["act"] += 1
                av = C["act"]
                def fR(e, pv=pv, pp=pp, mb2=m % 2, wd=C["dve"]):
                    e.wait_ge(sem["pe"], pv)
                    e.wait_ge(sem["dve"], wd)
                    e.activation(relu_t[mb2], pp[:, :], AF.Relu).then_inc(sem["act"], 1)
                em("scalar", fR)
                bank_guard[tag] = ("act", av)
                C["dve"] += 1
                def fH(e, av=av, mb2=m % 2, m=m):
                    e.wait_ge(sem["act"], av)
                    e.tensor_mul(h_sb[:, m * 512:(m + 1) * 512], relu_t[mb2],
                                 relu_t[mb2]).then_inc(sem["dve"], 1)
                em("vector", fH)
        for mp in range(8):
            mb = mp % 2
            g = slab_guard[("up", mb)]
            fr = dma_ld(
                upslab[mb].rearrange("p (kc n) -> p kc n", n=256),
                fc1_16.ap().rearrange("(kc p) n -> p kc n", p=128)[:, :, mp * 256:(mp + 1) * 256],
                pre=[("pe", g)])
            for s in range(2):
                m = 2 * mp + s
                tag = "p4" if m % 2 == 0 else "p5"
                pp = p4 if m % 2 == 0 else p5
                gd = bank_guard.get(tag)
                C["pe"] += 1
                pv = C["pe"]
                def fF1(e, mb=mb, s=s, pp=pp, fr=fr, gd=gd):
                    e.wait_ge(sem["ld"], fr)
                    if gd is not None:
                        e.wait_ge(sem[gd[0]], gd[1])
                    for dc in range(DC):
                        ins = e.matmul(pp[:, :],
                                       upslab[mb][:, dc * 256 + s * 128: dc * 256 + s * 128 + 128],
                                       x1b[:, dc * 512:(dc + 1) * 512],
                                       start=(dc == 0), stop=(dc == DC - 1))
                    ins.then_inc(sem["pe"], 1)
                em("tensor", fF1)
                slab_guard[("up", mb)] = pv
                C["act"] += 1
                av = C["act"]
                def fLa(e, pv=pv, pp=pp, m=m):
                    e.wait_ge(sem["pe"], pv)
                    e.activation(lat_sb[:, m * 512:(m + 1) * 512], pp[:, :],
                                 AF.Copy).then_inc(sem["act"], 1)
                em("scalar", fLa)
                bank_guard[tag] = ("act", av)
        act_lat_done = C["act"]

        for mp in range(8):
            mb = mp % 2
            g = slab_guard.get(("f2", mb)) if mp >= 2 else C["pe"]
            er = dma_ld(
                f2slab[mb].rearrange("p (kc n) -> p kc n", n=256),
                exp16.ap().rearrange("(kc p) n -> p kc n", p=128)[:, :, mp * 256:(mp + 1) * 256],
                pre=[("pe", g)])
            for s in range(2):
                m = 2 * mp + s
                tag = "p4" if m % 2 == 0 else "p5"
                pp = p4 if m % 2 == 0 else p5
                gd = bank_guard.get(tag)
                C["pe"] += 1
                pv = C["pe"]
                def fEx(e, mb=mb, s=s, pp=pp, er=er, gd=gd, wa=act_lat_done,
                        wd=dv_t2b):
                    e.wait_ge(sem["ld"], er)
                    e.wait_ge(sem["act"], wa)
                    e.wait_ge(sem["dve"], wd)
                    if gd is not None:
                        e.wait_ge(sem[gd[0]], gd[1])
                    for lc in range(16):
                        ins = e.matmul(pp[:, :],
                                       f2slab[mb][:, lc * 256 + s * 128: lc * 256 + s * 128 + 128],
                                       lat_sb[:, lc * 512:(lc + 1) * 512],
                                       start=(lc == 0), stop=(lc == 15))
                    ins.then_inc(sem["pe"], 1)
                em("tensor", fEx)
                slab_guard[("f2", mb)] = pv
                C["dve"] += 1
                dv = C["dve"]
                def fMo(e, pv=pv, pp=pp, m=m):
                    e.wait_ge(sem["pe"], pv)
                    e.tensor_mul(moe_sb[:, m * 512:(m + 1) * 512], pp[:, :],
                                 t2b).then_inc(sem["dve"], 1)
                em("vector", fMo)
                bank_guard[tag] = ("dve", dv)
        dv_moe_done = C["dve"]
        phase_mark(f"moe_{tile}")

        gp6, gp7 = bank_guard.get("p6"), bank_guard.get("p7")
        for jg in range(DC):
            mb = jg % 2
            gdw = slab_guard.get(("dw", mb)) if jg >= 2 else C["pe"]
            if gdw is None:
                gdw = C["pe"]
            dr = dma_ld(
                dwslab[mb].rearrange("p (kc n) -> p kc n", n=128),
                down16.ap().rearrange("(kc p) n -> p kc n", p=128)[:, :, jg * 128:(jg + 1) * 128],
                pre=[("pe", gdw), ("dve", dv_moe_done)] if jg < 2 else
                    [("pe", gdw)])
            if jg % 2 == 0:
                fb2 = (jg // 2) % 2
                gf2 = slab_guard.get(("f2", fb2)) if jg >= 4 else C["pe"]
                fr_pair = dma_ld(
                    f2slab[fb2].rearrange("p (kc n) -> p kc n", n=256),
                    fc2_16.ap().rearrange("(kc p) n -> p kc n", p=128)[:, :, (jg // 2) * 256:(jg // 2) * 256 + 256],
                    pre=[("pe", gf2)])
            fr = fr_pair
            tag = "p4" if mb == 0 else "p5"
            pp = p4 if mb == 0 else p5
            gd = bank_guard.get(tag)
            C["pe"] += 1
            pv = C["pe"]
            fb2 = (jg // 2) % 2
            fs = jg % 2
            def fD(e, mb=mb, pp=pp, dr=dr, fr=fr, gd=gd, wd=dv_moe_done,
                   fb2=fb2, fs=fs):
                e.wait_ge(sem["ld"], max(dr, fr))
                e.wait_ge(sem["dve"], wd)
                if gd is not None:
                    e.wait_ge(sem[gd[0]], gd[1])
                for ic in range(64):
                    e.matmul(pp[:, :], dwslab[mb][:, ic * 128:(ic + 1) * 128],
                             h_sb[:, ic * 512:(ic + 1) * 512],
                             start=(ic == 0), stop=False)
                for lc in range(16):
                    ins = e.matmul(pp[:, :],
                                   f2slab[fb2][:, lc * 256 + fs * 128: lc * 256 + fs * 128 + 128],
                                   moe_sb[:, lc * 512:(lc + 1) * 512],
                                   start=False, stop=(lc == 15))
                ins.then_inc(sem["pe"], 1)
            em("tensor", fD)
            slab_guard[("dw", mb)] = pv
            slab_guard[("f2", fb2)] = pv
            buf, guard = Sring[jg % 4], s_guard[jg % 4]
            C["act"] += 1
            av = C["act"]
            def gS(e, pv=pv, pp=pp, buf=buf, guard=guard):
                e.wait_ge(sem["pe"], pv)
                if guard is not None:
                    e.wait_ge(sem["st"], guard)
                e.activation(buf, pp[:, :], AF.Copy).then_inc(sem["act"], 1)
            em("scalar", gS)
            bank_guard[tag] = ("act", av)
            s_guard[jg % 4] = dma_st(sd[jg * 128:(jg + 1) * 128, t0:t0 + 512],
                                     buf, pre=[("act", av)])
            C["act"] += 1
            av2 = C["act"]
            def gSq2(e, av=av, buf=buf, sqb=sq[jg % 2],
                     wpe=(C["pe"] if jg < 2 else C["pe"] - 2)):
                e.wait_ge(sem["act"], av)
                e.wait_ge(sem["pe"], wpe)
                e.activation(sqb, buf, AF.Square).then_inc(sem["act"], 1)
            em("scalar", gSq2)
            C["pe"] += 1
            def fSt2(e, av2=av2, buf=buf, sqb=sq[jg % 2], gp6=gp6, gp7=gp7,
                     first=(jg == 0), last=(jg == DC - 1)):
                e.wait_ge(sem["act"], av2)
                if first and gp6 is not None:
                    e.wait_ge(sem[gp6[0]], gp6[1])
                if first and gp7 is not None:
                    e.wait_ge(sem[gp7[0]], gp7[1])
                e.matmul(p6, ones_f_sb, buf, start=first, stop=last)
                ins = e.matmul(p7, ones_f_sb, sqb, start=first, stop=last)
                ins.then_inc(sem["pe"], 1)
            em("tensor", fSt2)
        dv_st2 = fin_stats(mu2_sb, rstd2_sb, C["pe"])
        st_s_done[tile] = C["st"]
        phase_mark(f"down_{tile}")

        gp6, gp7 = bank_guard.get("p6"), bank_guard.get("p7")
        for jg in range(DC):
            gld = ln2_dve_guard.get(jg % 2)
            lS = dma_ld(Sld[jg % 2], sd[jg * 128:(jg + 1) * 128, t0:t0 + 512],
                        pre=([("st", st_s_done[tile])] if jg < 2 else []) +
                            ([("dve", gld)] if gld else []))
            lx = dma_ld(x1ld[jg % 2],
                        x1d[jg * 128:(jg + 1) * 128, t0:t0 + 512],
                        pre=[("st", st_x1_done[tile])] if jg < 2 else None)
            C["dve"] += 4
            dv = C["dve"]
            def fR2(e, jg=jg, lS=lS, lx=lx, dvs=dv_st2, sb=Sld[jg % 2],
                    xb=x1ld[jg % 2], base=dv - 4):
                e.wait_ge(sem["ld"], max(lS, lx))
                e.wait_ge(sem["dve"], dvs)
                dst = r2[:, jg * 512:(jg + 1) * 512]
                e.tensor_sub(dst, sb, mu2_sb).then_inc(sem["dve"], 1)
                e.wait_ge(sem["dve"], base + 1)
                e.tensor_mul(dst, dst, rstd2_sb).then_inc(sem["dve"], 1)
                e.wait_ge(sem["dve"], base + 2)
                e.tensor_scalar(dst, dst, g_sb[:, jg:jg + 1], b_sb[:, jg:jg + 1],
                                OP.mult, OP.add).then_inc(sem["dve"], 1)
                e.wait_ge(sem["dve"], base + 3)
                e.tensor_add(dst, dst, xb).then_inc(sem["dve"], 1)
            em("vector", fR2)
            ln2_dve_guard[jg % 2] = dv
            C["act"] += 1
            av3 = C["act"]
            def gSq3(e, dv=dv, jg=jg, sqb=sq[jg % 2],
                     wpe=(C["pe"] if jg < 2 else C["pe"] - 1)):
                e.wait_ge(sem["dve"], dv)
                e.wait_ge(sem["pe"], wpe)
                e.activation(sqb, r2[:, jg * 512:(jg + 1) * 512],
                             AF.Square).then_inc(sem["act"], 1)
            em("scalar", gSq3)
            C["pe"] += 1
            def fSt3(e, av3=av3, jg=jg, sqb=sq[jg % 2], gp6=gp6, gp7=gp7,
                     first=(jg == 0), last=(jg == DC - 1)):
                e.wait_ge(sem["act"], av3)
                if first and gp6 is not None:
                    e.wait_ge(sem[gp6[0]], gp6[1])
                if first and gp7 is not None:
                    e.wait_ge(sem[gp7[0]], gp7[1])
                e.matmul(p6, ones_f_sb, r2[:, jg * 512:(jg + 1) * 512],
                         start=first, stop=last)
                ins = e.matmul(p7, ones_f_sb, sqb, start=first, stop=last)
                ins.then_inc(sem["pe"], 1)
            em("tensor", fSt3)
        dv_st3 = fin_stats(mu3_sb, rstd3_sb, C["pe"])

        for jg in range(DC):
            C["dve"] += 2
            def fX2(e, jg=jg, dvs=dv_st3, dv=C["dve"]):
                e.wait_ge(sem["dve"], dvs)
                dst = r2[:, jg * 512:(jg + 1) * 512]
                e.tensor_sub(dst, dst, mu3_sb).then_inc(sem["dve"], 1)
                e.wait_ge(sem["dve"], dv - 1)
                e.tensor_mul(dst, dst, rstd3_sb).then_inc(sem["dve"], 1)
            em("vector", fX2)
        dv_x2_done = C["dve"]
        for t1 in range(4):
            for jg in range(DC):
                tag = "p6" if (jg % 2 == 0) else "p7"
                pp = p6 if (jg % 2 == 0) else p7
                gd = bank_guard.get(tag)
                C["pe"] += 1
                pv = C["pe"]
                def fTr2(e, jg=jg, t1=t1, pp=pp, gd=gd, wd=dv_x2_done):
                    e.wait_ge(sem["dve"], wd)
                    if gd is not None:
                        e.wait_ge(sem[gd[0]], gd[1])
                    e.matmul(pp[:, 0:128],
                             r2[:, jg * 512 + t1 * 128: jg * 512 + t1 * 128 + 128],
                             ident_sb, is_transpose=True, start=True,
                             stop=True).then_inc(sem["pe"], 1)
                em("tensor", fTr2)
                C["act"] += 1
                av = C["act"]
                def gE(e, pv=pv, jg=jg, pp=pp, og=os_guard[0] if jg == 0 else None):
                    e.wait_ge(sem["pe"], pv)
                    if og is not None:
                        e.wait_ge(sem["st"], og)
                    e.activation(osb[:, jg * 128:(jg + 1) * 128], pp[:, 0:128],
                                 AF.Copy).then_inc(sem["act"], 1)
                em("scalar", gE)
                bank_guard[tag] = ("act", av)
            os_guard[0] = dma_st(out_rows[t0 + t1 * 128: t0 + (t1 + 1) * 128, :],
                                 osb, pre=[("act", C["act"])])

    phase_mark("all")

    if stop_after is not None:
        lens, cc = marks[stop_after]
        for e in ENGS:
            Q[e] = Q[e][:lens[e]]
        em("sync", lambda e, v=cc["ld"]: e.wait_ge(sem["ld"], v))
        em("scalar", lambda e, v=cc["act"]: e.wait_ge(sem["act"], v))
        em("vector", lambda e, v=cc["dve"]: e.wait_ge(sem["dve"], v))
        em("tensor", lambda e, v=cc["pe"]: e.wait_ge(sem["pe"], v))
        C["st"] = cc["st"]
        if debug_dump in ("hsb", "latsb", "moesb", "x1bsb", "t2bsb"):
            bufmap = {"hsb": (h_sb, 32768), "latsb": (lat_sb, 8192),
                      "moesb": (moe_sb, 8192), "x1bsb": (x1b, 16384),
                      "t2bsb": (t2b, 512)}
            bb, ncols = bufmap[debug_dump]
            ov = out_rows.ap().rearrange("(b p) d -> p b d", p=128)
            def fdump2(e, v=cc["cst"], w=cc["st"], bb=bb, ncols=ncols, ov=ov):
                e.wait_ge(sem["cst"], v)
                e.wait_ge(sem["st"], w)
                e.wait_ge(sem["dve"], cc["dve"])
                e.wait_ge(sem["act"], cc["act"])
                nb = (ncols + 4095) // 4096
                e.dma_start(out=ov[:, 0:nb, 0:min(ncols, 4096)],
                            in_=bb[:, 0:ncols].rearrange("p (b d) -> p b d", b=nb)
                            if ncols > 4096 else bb[:, 0:ncols].rearrange("p d -> p 1 d" if False else "p (b d) -> p b d", b=1)
                            ).then_inc(sem["cst"], 16)
                e.wait_ge(sem["cst"], v + 16)
            em("gpsimd", fdump2)
            debug_dump = None
        if debug_dump is not None:
            src_t = {"kd": kd, "vd": vd, "att": att_d, "x1": x1d, "sdump": sd,
                     "x16": x16, "q": None}[debug_dump]
            def fdump(e, v=cc["cst"], w=cc["st"]):
                e.wait_ge(sem["cst"], v)
                e.wait_ge(sem["st"], w)
                if debug_dump == "kd":
                    e.dma_start(out=out_rows[:, :], in_=kd[0:1024, 0:4096]).then_inc(sem["cst"], 16)
                elif debug_dump == "vd":
                    e.dma_start(out=out_rows[:, :], in_=vd[0:1024, 0:4096]).then_inc(sem["cst"], 16)
                elif debug_dump == "x16":
                    e.dma_start(out=out_rows[:, :], in_=x16[0:1024, 0:4096]).then_inc(sem["cst"], 16)
                elif debug_dump == "att":
                    e.dma_start(out=out_rows[0:1024, 0:1024], in_=att_d[0:1024, :]).then_inc(sem["cst"], 16)
                elif debug_dump == "x1":
                    e.dma_start(out=out_rows[0:1024, 0:1024], in_=x1d[0:1024, :]).then_inc(sem["cst"], 16)
                elif debug_dump == "sdump":
                    e.dma_start(out=out_rows[0:1024, 0:1024], in_=sd[0:1024, :]).then_inc(sem["cst"], 16)
                e.wait_ge(sem["cst"], v + 16)
            em("gpsimd", fdump)
        else:
            em("gpsimd", lambda e, v=cc["cst"]: e.wait_ge(sem["cst"], v))

    em("scalar", lambda e, v=C["st"]: e.wait_ge(sem["st"], v))

    @block.sync
    def _(e):
        for f in Q["sync"]:
            f(e)

    @block.gpsimd
    def _(e):
        for f in Q["gpsimd"]:
            f(e)

    @block.scalar
    def _(e):
        for f in Q["scalar"]:
            f(e)

    @block.vector
    def _(e):
        for f in Q["vector"]:
            f(e)

    @block.tensor
    def _(e):
        for f in Q["tensor"]:
            f(e)

    ctx.close()
    return nc


def _host_inputs(inputs):
    x = np.ascontiguousarray(np.asarray(inputs["L0_attn_x"], dtype=np.float32)[0])
    fm = lambda v: np.ascontiguousarray(
        np.asarray(v, dtype=np.float32).reshape(DC, 128).T)
    base = {
        "x": x,
        "wq": np.ascontiguousarray(np.asarray(inputs["L0_attn_Wq"], np.float32)),
        "wk": np.ascontiguousarray(np.asarray(inputs["L0_attn_Wk"], np.float32)),
        "wv": np.ascontiguousarray(np.asarray(inputs["L0_attn_Wv"], np.float32)),
        "wo": np.ascontiguousarray(np.asarray(inputs["L0_attn_Wo"], np.float32)),
        "bq_fm": fm(inputs["L0_attn_bq"]),
        "bk_fm": fm(inputs["L0_attn_bk"]),
        "bo_fm": fm(inputs["L0_attn_bo"]),
        "g_fm": fm(inputs["L0_ffn_ln_gamma"]),
        "b_fm": fm(inputs["L0_ffn_ln_beta"]),
        "bv_row": np.ascontiguousarray(
            np.asarray(inputs["L0_attn_bv"], np.float32).reshape(1, D)),
        "gate_w": np.ascontiguousarray(np.asarray(inputs["L0_ffn_gate_w"], np.float32)),
        "up_w": np.ascontiguousarray(np.asarray(inputs["L0_ffn_up_w"], np.float32)),
        "down_w": np.ascontiguousarray(np.asarray(inputs["L0_ffn_down_w"], np.float32)),
        "fc1_w": np.ascontiguousarray(np.asarray(inputs["L0_ffn_fc1_w"], np.float32)),
        "fc2_w": np.ascontiguousarray(np.asarray(inputs["L0_ffn_fc2_w"], np.float32)),
        "exp_w": np.ascontiguousarray(np.asarray(inputs["L0_ffn_expert_w"], np.float32)),
    }
    in_maps, row_idx = [], []
    for c in range(NC):
        rows = np.concatenate([np.arange((8 * j + c) * QW, (8 * j + c + 1) * QW)
                               for j in range(NSLOT)])
        row_idx.append(rows)
        mk = np.zeros((NSLOT, 16, 128, QW), dtype=np.float32)
        for j in range(NSLOT):
            qpos = (8 * j + c) * QW + np.arange(QW)[None, :]
            for m in range(16):
                kpos = (16 * j + m) * 128 + np.arange(128)[:, None]
                mk[j, m] = (kpos <= qpos)
        m = dict(base)
        m["xq"] = np.ascontiguousarray(x[rows])
        m["masks"] = mk.astype(ml_dtypes.bfloat16)
        in_maps.append(m)
    return in_maps, row_idx


def kernel(**inputs):
    in_maps, row_idx = _host_inputs(inputs)
    nc = build_nc()
    res = run_bass_kernel_spmd(nc, in_maps, core_ids=list(range(NC)))
    out = np.empty((T, D), dtype=np.float32)
    for c in range(NC):
        out[row_idx[c]] = res.results[c]["out_rows"]
    return out.reshape(1, T, D)


# version 08 (final)
# speedup vs baseline: 1.4918x; 1.4918x over previous
import math
import numpy as np
import ml_dtypes
import concourse.bass as bass
import concourse.mybir as mybir
from concourse.bass_utils import run_bass_kernel_spmd

F32 = mybir.dt.float32
BF16 = mybir.dt.bfloat16
U8 = mybir.dt.uint8
AF = mybir.ActivationFunctionType
OP = mybir.AluOpType
AX = mybir.AxisListType.X

NC = 8
D = 4096
T = 8192
TQ = 1024
H = 32
INT = 8192
LAT = 2048
NE = 8
LN_EPS = 1e-5
ISQ = 1.0 / math.sqrt(128.0)
DC = 32
NSLOT = 4
QW = 256

ARENA_B = 176 * 1024


def build_nc(stop_after=None, debug_dump=None):
    nc = bass.Bass("TRN2", target_bir_lowering=False)

    xkv = nc.dram_tensor("xkv", [TQ, D], F32, kind="ExternalInput")
    xq = nc.dram_tensor("xq", [TQ, D], F32, kind="ExternalInput")
    wq = nc.dram_tensor("wq", [D, D], F32, kind="ExternalInput")
    wk = nc.dram_tensor("wk", [D, D], F32, kind="ExternalInput")
    wv = nc.dram_tensor("wv", [D, D], F32, kind="ExternalInput")
    wo = nc.dram_tensor("wo", [D, D], F32, kind="ExternalInput")
    bq_fm = nc.dram_tensor("bq_fm", [128, DC], F32, kind="ExternalInput")
    bk_fm = nc.dram_tensor("bk_fm", [128, DC], F32, kind="ExternalInput")
    bo_fm = nc.dram_tensor("bo_fm", [128, DC], F32, kind="ExternalInput")
    g_fm = nc.dram_tensor("g_fm", [128, DC], F32, kind="ExternalInput")
    b_fm = nc.dram_tensor("b_fm", [128, DC], F32, kind="ExternalInput")
    bv_row = nc.dram_tensor("bv_row", [1, D], F32, kind="ExternalInput")
    gate_w = nc.dram_tensor("gate_w", [D, NE], F32, kind="ExternalInput")
    up_w = nc.dram_tensor("up_w", [D, INT], F32, kind="ExternalInput")
    down_w = nc.dram_tensor("down_w", [INT, D], F32, kind="ExternalInput")
    fc1_w = nc.dram_tensor("fc1_w", [D, LAT], F32, kind="ExternalInput")
    fc2_w = nc.dram_tensor("fc2_w", [LAT, D], F32, kind="ExternalInput")
    exp_w = nc.dram_tensor("exp_w", [LAT, LAT], F32, kind="ExternalInput")
    masks = nc.dram_tensor("masks", [NSLOT, 16, 128, QW], BF16, kind="ExternalInput")
    out_rows = nc.dram_tensor("out_rows", [TQ, D], F32, kind="ExternalOutput")

    ones_b = nc.inline_tensor(np.ones((128, 128), dtype=ml_dtypes.bfloat16), "ones_b")
    ones_f = nc.inline_tensor(np.ones((128, 128), dtype=np.float32), "ones_f")
    ident_f = nc.inline_tensor(np.eye(128, dtype=np.float32), "ident_f")
    eps_t = nc.inline_tensor(np.full((128, 1), LN_EPS, dtype=np.float32), "eps_t")

    xkv16 = nc.dram_tensor("xkv16", [TQ, D], BF16, kind="Internal")
    xq16 = nc.dram_tensor("xq16", [TQ, D], BF16, kind="Internal")
    wq16 = nc.dram_tensor("wq16", [D, D], BF16, kind="Internal")
    wk16 = nc.dram_tensor("wk16", [D, D], BF16, kind="Internal")
    wv16 = nc.dram_tensor("wv16", [D, D], BF16, kind="Internal")
    wo16 = nc.dram_tensor("wo16", [D, D], BF16, kind="Internal")
    kd_part = nc.dram_tensor("kd_part", [D, TQ], BF16, kind="Internal")
    vd_part = nc.dram_tensor("vd_part", [TQ, D], BF16, kind="Internal")
    kd_all = nc.dram_tensor("kd_all", [NC * D, TQ], BF16, kind="Internal")
    vd_all = nc.dram_tensor("vd_all", [T, D], BF16, kind="Internal")
    att_d = nc.dram_tensor("att_d", [D, TQ], BF16, kind="Internal")
    x1d = nc.dram_tensor("x1d", [D, TQ], F32, kind="Internal")
    sd = nc.dram_tensor("sd", [D, TQ], F32, kind="Internal")
    up16 = nc.dram_tensor("up16", [D, INT], BF16, kind="Internal")
    down16 = nc.dram_tensor("down16", [INT, D], BF16, kind="Internal")
    fc1_16 = nc.dram_tensor("fc1_16", [D, LAT], BF16, kind="Internal")
    fc2_16 = nc.dram_tensor("fc2_16", [LAT, D], BF16, kind="Internal")
    exp16 = nc.dram_tensor("exp16", [LAT, LAT], BF16, kind="Internal")

    ENGS = ("sync", "gpsimd", "scalar", "vector", "tensor")
    Q = {e: [] for e in ENGS}
    C = {"ld": 0, "cst": 0, "st": 0, "pe": 0, "act": 0, "dve": 0,
         "cck": 0, "ccv": 0}
    marks = {}

    def phase_mark(name):
        marks[name] = ({e: len(Q[e]) for e in ENGS}, dict(C))

    import contextlib
    ctx = contextlib.ExitStack()
    arena_t = ctx.enter_context(nc.sbuf_tensor("arena", [128, ARENA_B], U8))
    pA = ctx.enter_context(nc.psum_tensor("pA", [128, 1024], F32))
    pB = ctx.enter_context(nc.psum_tensor("pB", [128, 1024], F32))
    p4 = ctx.enter_context(nc.psum_tensor("p4", [128, 512], F32))
    p5 = ctx.enter_context(nc.psum_tensor("p5", [128, 512], F32))
    pC = ctx.enter_context(nc.psum_tensor("pC", [128, 1024], F32))
    p6 = pC[:, 0:512]
    p7 = pC[:, 512:1024]
    sem = {k: ctx.enter_context(nc.semaphore(k)) for k in C}
    block = ctx.enter_context(nc.Block())

    def em(eng, fn):
        Q[eng].append(fn)

    def view(off, dtype, cols, p0=0, parts=128):
        esz = mybir.dt.size(dtype)
        assert off % 64 == 0 and off + cols * esz <= ARENA_B, (off, cols, dtype)
        return arena_t[p0:p0 + parts, off:off + cols * esz].bitcast(dtype)

    bank_guard = {}

    def pe_wait_bank(tag):
        g = bank_guard.get(tag)
        if g is None:
            return None
        return g

    ones_b_sb = view(0, BF16, 128)
    ones_f_sb = view(320, F32, 128)
    ident_sb = view(832, F32, 128)
    bq_sb = view(1344, F32, DC)
    bk_sb = view(1472, F32, DC)
    bo_sb = view(1600, F32, DC)
    g_sb = view(1728, F32, DC)
    b_sb = view(1856, F32, DC)
    gate_sb = view(1984, BF16, DC * NE)
    eps_sb = view(2496, F32, 1)
    BASE = 4096

    def dma_ld(out_ap, in_ap, pre=None):
        C["ld"] += 16
        v = C["ld"]
        def f(e, o=out_ap, i=in_ap, pre=tuple(pre or ())):
            for s, vv in pre:
                e.wait_ge(sem[s], vv)
            e.dma_start(out=o, in_=i).then_inc(sem["ld"], 16)
        em("sync", f)
        return v

    def dma_ldx(out_ap, in_ap, pre=None):
        C["ld"] += 16
        v = C["ld"]
        def f(e, o=out_ap, i=in_ap, pre=tuple(pre or ())):
            for s, vv in pre:
                e.wait_ge(sem[s], vv)
            e.dma_start_transpose(o, i).then_inc(sem["ld"], 16)
        em("sync", f)
        return v

    def dma_cst(out_ap, in_ap, pre=None):
        C["cst"] += 16
        v = C["cst"]
        def f(e, o=out_ap, i=in_ap, pre=tuple(pre or ())):
            for s, vv in pre:
                e.wait_ge(sem[s], vv)
            e.dma_start(out=o, in_=i).then_inc(sem["cst"], 16)
        em("gpsimd", f)
        return v

    def dma_st(out_ap, in_ap, pre=None):
        C["st"] += 16
        v = C["st"]
        def f(e, o=out_ap, i=in_ap, pre=tuple(pre or ())):
            for s, vv in pre:
                e.wait_ge(sem[s], vv)
            e.dma_start(out=o, in_=i).then_inc(sem["st"], 16)
        em("scalar", f)
        return v

    for r in range(4):
        dma_cst(wk16[r * 1024:(r + 1) * 1024, :], wk[r * 1024:(r + 1) * 1024, :])
    cst_wk = C["cst"]
    cst_xchunk = []
    for r in range(2):
        dma_cst(xkv16[r * 512:(r + 1) * 512, :], xkv[r * 512:(r + 1) * 512, :])
        cst_xchunk.append(C["cst"])
    for r in range(4):
        dma_cst(wv16[r * 1024:(r + 1) * 1024, :], wv[r * 1024:(r + 1) * 1024, :])
    cst_wkv = C["cst"]
    cst_x = cst_xchunk[-1]
    dma_cst(xq16[:, :], xq[:, :])
    cst_xq = C["cst"]
    for r in range(4):
        dma_cst(wq16[r * 1024:(r + 1) * 1024, :], wq[r * 1024:(r + 1) * 1024, :])
    cst_wq = C["cst"]
    for r in range(4):
        dma_cst(wo16[r * 1024:(r + 1) * 1024, :], wo[r * 1024:(r + 1) * 1024, :])
    cst_wo = C["cst"]
    dma_cst(gate_sb.rearrange("p (kc e) -> p kc e", e=NE),
            gate_w.ap().rearrange("(kc p) e -> p kc e", p=128))
    cst_gate = C["cst"]
    for r in range(8):
        dma_cst(up16[r * 512:(r + 1) * 512, :], up_w[r * 512:(r + 1) * 512, :])
    for r in range(8):
        dma_cst(down16[r * 1024:(r + 1) * 1024, :], down_w[r * 1024:(r + 1) * 1024, :])
    for r in range(2):
        dma_cst(fc1_16[r * 2048:(r + 1) * 2048, :], fc1_w[r * 2048:(r + 1) * 2048, :])
        dma_cst(fc2_16[r * 1024:(r + 1) * 1024, :], fc2_w[r * 1024:(r + 1) * 1024, :])
    dma_cst(exp16[:, :], exp_w[:, :])
    cst_ffn = C["cst"]

    dma_ld(ones_b_sb, ones_b[:, :])
    dma_ld(ones_f_sb, ones_f[:, :])
    dma_ld(ident_sb, ident_f[:, :])
    dma_ld(bq_sb, bq_fm[:, :])
    dma_ld(bk_sb, bk_fm[:, :])
    dma_ld(bo_sb, bo_fm[:, :])
    dma_ld(g_sb, g_fm[:, :])
    dma_ld(b_sb, b_fm[:, :])
    dma_ld(eps_sb, eps_t[:, :])
    ld_consts = C["ld"]
    phase_mark("p0")

    slabA_o, slabB_o, xfm_o = BASE, BASE + 65536, BASE + 131072
    bvb_o = xfm_o + 32768
    rrow_o = bvb_o + 8192
    ering_o = rrow_o + 2048
    slabs = [view(slabA_o, BF16, DC * 1024), view(slabB_o, BF16, DC * 1024)]
    xfm = view(xfm_o, BF16, DC * 512)
    bvb = view(bvb_o, F32, 2048)
    Rrow = view(rrow_o, F32, 512)
    ering = [view(ering_o + i * 1024, BF16, 512) for i in range(2)]
    er_guard = [None] * 2
    er_i = 0

    pairs = [("k", 0, 1), ("k", 2, 3), ("v", 0, 1), ("v", 2, 3)]
    slab_pe_done = 0
    xfm_pe_done = 0

    for pi, (mat, ibA, ibB) in enumerate(pairs):
        w16t = wk16 if mat == "k" else wv16
        pre = []
        if pi == 0:
            pre = [("cst", cst_wk)]
        elif pi == 2:
            pre = [("cst", cst_wkv)]
        pre = pre + ([("pe", slab_pe_done)] if slab_pe_done else [])
        slab_lds = []
        for si, ib in ((0, ibA), (1, ibB)):
            for dc in range(DC):
                slab_lds.append(dma_ldx(
                    slabs[si][:, dc * 1024:(dc + 1) * 1024],
                    w16t[ib * 1024:(ib + 1) * 1024, dc * 128:(dc + 1) * 128],
                    pre=pre if (si == 0 and dc == 0) else None))
        slab_ready = slab_lds[-1]

        if mat == "v":
            rrow_pe_guard = 0
            for half in range(4):
                i0 = ibA * 1024 + half * 512
                C["dve"] += 1
                dz = C["dve"]
                def fz(e, g=rrow_pe_guard):
                    if g:
                        e.wait_ge(sem["pe"], g)
                    e.memset(Rrow, 0.0).then_inc(sem["dve"], 1)
                em("vector", fz)
                lv = dma_ld(Rrow[0:1, 0:512], bv_row[0:1, i0:i0 + 512],
                            pre=[("dve", dz)])
                C["pe"] += 1
                pv = C["pe"]
                gp6 = bank_guard.get("p6")
                def fb(e, lv=lv, gp6=gp6):
                    e.wait_ge(sem["ld"], lv)
                    if gp6 is not None:
                        e.wait_ge(sem[gp6[0]], gp6[1])
                    e.matmul(p6, ones_f_sb, Rrow, start=True,
                             stop=True).then_inc(sem["pe"], 1)
                em("tensor", fb)
                rrow_pe_guard = pv
                C["dve"] += 1
                dv = C["dve"]
                def fc(e, pv=pv, half=half):
                    e.wait_ge(sem["pe"], pv)
                    e.tensor_copy(bvb[:, half * 512:(half + 1) * 512],
                                  p6).then_inc(sem["dve"], 1)
                em("vector", fc)
                bank_guard["p6"] = ("dve", dv)

        for tt in range(2):
            xpre = [("cst", cst_xchunk[tt])] if pi == 0 else []
            if xfm_pe_done:
                xpre.append(("pe", xfm_pe_done))
            xlds = []
            for dc in range(DC):
                xlds.append(dma_ldx(
                    xfm[:, dc * 512:(dc + 1) * 512],
                    xkv16[tt * 512:(tt + 1) * 512, dc * 128:(dc + 1) * 128],
                    pre=xpre if dc == 0 else None))
            x_ready = xlds[-1]

            if mat == "k":
                for si, ib in ((0, ibA), (1, ibB)):
                    for ic in range(8):
                        ig = ib * 8 + ic
                        tag = "p4" if (ic % 2 == 0) else "p5"
                        pp = p4 if (ic % 2 == 0) else p5
                        gd = bank_guard.get(tag)
                        C["pe"] += 1
                        pv = C["pe"]
                        def fK(e, si=si, ic=ic, pp=pp, x_ready=x_ready,
                               slab_ready=slab_ready, gd=gd):
                            e.wait_ge(sem["ld"], max(x_ready, slab_ready))
                            if gd is not None:
                                e.wait_ge(sem[gd[0]], gd[1])
                            for dc in range(DC):
                                ins = e.matmul(
                                    pp[:, :],
                                    slabs[si][:, dc * 1024 + ic * 128: dc * 1024 + ic * 128 + 128],
                                    xfm[:, dc * 512:(dc + 1) * 512],
                                    start=(dc == 0), stop=(dc == DC - 1))
                            ins.then_inc(sem["pe"], 1)
                        em("tensor", fK)
                        buf, guard = ering[er_i % 2], er_guard[er_i % 2]
                        C["act"] += 1
                        av = C["act"]
                        def gK(e, pv=pv, pp=pp, buf=buf, ig=ig, guard=guard):
                            e.wait_ge(sem["pe"], pv)
                            if guard is not None:
                                e.wait_ge(sem["st"], guard)
                            e.activation(buf, pp[:, :], AF.Identity,
                                         bias=bk_sb[:, ig:ig + 1],
                                         scale=1.0).then_inc(sem["act"], 1)
                        em("scalar", gK)
                        bank_guard[tag] = ("act", av)
                        er_guard[er_i % 2] = dma_st(
                            kd_part[ig * 128:(ig + 1) * 128, tt * 512:(tt + 1) * 512],
                            buf, pre=[("act", av)])
                        er_i += 1
            else:
                for si, ib in ((0, ibA), (1, ibB)):
                    for nb in range(2):
                        for t1 in range(4):
                            k2 = (nb * 4 + t1) % 2
                            tag = "p4" if k2 == 0 else "p5"
                            pp = p4 if k2 == 0 else p5
                            gd = bank_guard.get(tag)
                            C["pe"] += 1
                            pv = C["pe"]
                            def fV(e, si=si, nb=nb, t1=t1, pp=pp, gd=gd,
                                   x_ready=x_ready, slab_ready=slab_ready):
                                e.wait_ge(sem["ld"], max(x_ready, slab_ready))
                                if gd is not None:
                                    e.wait_ge(sem[gd[0]], gd[1])
                                for dc in range(DC):
                                    ins = e.matmul(
                                        pp[:, :],
                                        xfm[:, dc * 512 + t1 * 128: dc * 512 + t1 * 128 + 128],
                                        slabs[si][:, dc * 1024 + nb * 512: dc * 1024 + nb * 512 + 512],
                                        start=(dc == 0), stop=(dc == DC - 1))
                                ins.then_inc(sem["pe"], 1)
                            em("tensor", fV)
                            buf, guard = ering[er_i % 2], er_guard[er_i % 2]
                            C["dve"] += 1
                            dv = C["dve"]
                            boff = (ib - ibA) * 1024 + nb * 512
                            def gV(e, pv=pv, pp=pp, buf=buf, boff=boff, guard=guard):
                                e.wait_ge(sem["pe"], pv)
                                if guard is not None:
                                    e.wait_ge(sem["st"], guard)
                                e.tensor_add(buf, pp[:, :],
                                             bvb[:, boff:boff + 512]).then_inc(sem["dve"], 1)
                            em("vector", gV)
                            bank_guard[tag] = ("dve", dv)
                            tok0 = tt * 512 + t1 * 128
                            col0 = ib * 1024 + nb * 512
                            er_guard[er_i % 2] = dma_st(
                                vd_part[tok0:tok0 + 128, col0:col0 + 512], buf,
                                pre=[("dve", dv)])
                            er_i += 1
            xfm_pe_done = C["pe"]
            if tt == 0:
                phase_mark(f"{mat}{pi}tt0")
        slab_pe_done = C["pe"]
        phase_mark(f"{mat}{pi}")
    st_kv_done = C["st"]

    C["cck"] += 1
    C["ccv"] += 1
    def fCC(e, wst=st_kv_done):
        e.wait_ge(sem["st"], wst)
        e.collective_compute(
            "AllGather", mybir.AluOpType.bypass,
            replica_groups=[list(range(NC))],
            ins=[kd_part.ap().opt()],
            outs=[kd_all.ap().opt()]).then_inc(sem["cck"], 1)
        e.collective_compute(
            "AllGather", mybir.AluOpType.bypass,
            replica_groups=[list(range(NC))],
            ins=[vd_part.ap().opt()],
            outs=[vd_all.ap().opt()]).then_inc(sem["ccv"], 1)
    em("gpsimd", fCC)
    phase_mark("p1")

    qsb_o = BASE
    qslab_o = BASE + 65536
    xqfm_o = BASE + 131072
    Q_sb = view(qsb_o, BF16, DC * 1024)
    qslab = view(qslab_o, BF16, DC * 1024)
    xqfm = view(xqfm_o, BF16, DC * 512)
    p1_pe_done = C["pe"]
    p1_act_done = C["act"]
    p1_dve_done = C["dve"]

    qslab_pe_done = 0
    xq_pe_done = 0
    for ib in range(4):
        pre = [("cst", cst_wq)] if ib == 0 else []
        pre.append(("pe", qslab_pe_done if qslab_pe_done else p1_pe_done))
        sl = []
        for dc in range(DC):
            sl.append(dma_ldx(
                qslab[:, dc * 1024:(dc + 1) * 1024],
                wq16[ib * 1024:(ib + 1) * 1024, dc * 128:(dc + 1) * 128],
                pre=pre if dc == 0 else None))
        slab_ready = sl[-1]
        for qh in range(2):
            xpre = [("cst", cst_xq)] if (ib == 0 and qh == 0) else []
            xpre.append(("pe", xq_pe_done if xq_pe_done else p1_pe_done))
            xl = []
            for dc in range(DC):
                xl.append(dma_ldx(
                    xqfm[:, dc * 512:(dc + 1) * 512],
                    xq16[qh * 512:(qh + 1) * 512, dc * 128:(dc + 1) * 128],
                    pre=xpre if dc == 0 else None))
            x_ready = xl[-1]
            for ic in range(8):
                ig = ib * 8 + ic
                tag = "p4" if (ic % 2 == 0) else "p5"
                pp = p4 if (ic % 2 == 0) else p5
                gd = bank_guard.get(tag)
                C["pe"] += 1
                pv = C["pe"]
                def fQ(e, ic=ic, pp=pp, gd=gd, x_ready=x_ready, slab_ready=slab_ready):
                    e.wait_ge(sem["ld"], max(x_ready, slab_ready))
                    if gd is not None:
                        e.wait_ge(sem[gd[0]], gd[1])
                    for dc in range(DC):
                        ins = e.matmul(
                            pp[:, :],
                            qslab[:, dc * 1024 + ic * 128: dc * 1024 + ic * 128 + 128],
                            xqfm[:, dc * 512:(dc + 1) * 512],
                            start=(dc == 0), stop=(dc == DC - 1))
                    ins.then_inc(sem["pe"], 1)
                em("tensor", fQ)
                C["act"] += 1
                av = C["act"]
                def gQ(e, pv=pv, pp=pp, ig=ig, qh=qh):
                    e.wait_ge(sem["pe"], pv)
                    e.activation(
                        Q_sb[:, ig * 1024 + qh * 512: ig * 1024 + qh * 512 + 512],
                        pp[:, :], AF.Identity, bias=bq_sb[:, ig:ig + 1],
                        scale=1.0).then_inc(sem["act"], 1)
                em("scalar", gQ)
                bank_guard[tag] = ("act", av)
            xq_pe_done = C["pe"]
        qslab_pe_done = C["pe"]
    bank_guard["p45"] = ("act", C["act"])
    phase_mark("p2")

    mask_o = qslab_o
    kh_o = mask_o + 32768
    vh_o = xqfm_o
    pt_o = bvb_o
    rc_o = pt_o + 4096
    attb_o = rc_o + 1024 + 1024
    mask_sb = view(mask_o, BF16, NSLOT * 16 * QW)
    kh = [view(kh_o + i * 16384, BF16, T) for i in range(2)]
    vh = [view(vh_o + i * 16384, BF16, T) for i in range(2)]
    pt = [view(pt_o, BF16, 1024), view(pt_o + 2048, BF16, 1024),
          view(176128, BF16, 1024)]
    rc_sb = view(rc_o, F32, QW)
    attb = [view(attb_o + i * 512, BF16, QW) for i in range(4)]

    mld = []
    for j in range(NSLOT):
        mld.append(dma_ld(
            mask_sb[:, j * 16 * QW:(j + 1) * 16 * QW].rearrange(
                "p (m q) -> p m q", q=QW),
            masks[j, :, :, :].rearrange("m p q -> p m q"),
            pre=[("pe", qslab_pe_done)] if j == 0 else None))
    ld_mask = mld[-1]

    def mask_view(j, m):
        o = (j * 16 + m) * QW
        return mask_sb[:, o:o + QW]

    kv_ready = [0, 0]
    kv_guard = [0, 0]
    att_guard = [None] * 4
    pt_guard = [None, None, None]
    g_idx = 0
    for h in range(H):
        hb = h % 2
        pre = [("cck", 1), ("ccv", 1), ("pe", xq_pe_done)] if h == 0 else []
        if kv_guard[hb]:
            pre.append(("pe", kv_guard[hb]))
        dma_ld(kh[hb].rearrange("p (r t) -> p r t", r=NC),
               kd_all.ap().rearrange("(r d) t -> d r t", r=NC)[
                   h * 128:(h + 1) * 128, :, :],
               pre=pre)
        kv_ready[hb] = dma_ld(
            vh[hb].rearrange("p (kb n) -> p kb n", n=128),
            vd_all.ap().rearrange("(kb p) i -> p kb i", p=128)[:, :, h * 128:(h + 1) * 128])

        for j in range(NSLOT):
            ngrp = (j + 1) * 4
            for g in range(ngrp):
                sb_i = g_idx % 3
                ppS = (pA, pB, pC)[sb_i]
                tagS = ("pA", "pB", "pC")[sb_i]
                ptb = pt[sb_i]
                gd = bank_guard.get(tagS)
                C["pe"] += 1
                pv = C["pe"]
                def fS(e, h=h, j=j, g=g, ppS=ppS, hb=hb, gd=gd,
                       kvr=kv_ready[hb], lm=ld_mask):
                    e.wait_ge(sem["ld"], max(kvr, lm))
                    if gd is not None:
                        e.wait_ge(sem[gd[0]], gd[1])
                    for u in range(4):
                        kb = g * 4 + u
                        ins = e.matmul(
                            ppS[:, u * 256:(u + 1) * 256],
                            kh[hb][:, kb * 128:(kb + 1) * 128],
                            Q_sb[:, h * 1024 + j * 256: h * 1024 + j * 256 + 256],
                            start=True, stop=True)
                    ins.then_inc(sem["pe"], 1)
                em("tensor", fS)
                ptg = pt_guard[sb_i]
                C["act"] += 1
                av = C["act"]
                def fE(e, pv=pv, ppS=ppS, ptb=ptb, ptg=ptg):
                    e.wait_ge(sem["pe"], pv)
                    if ptg is not None:
                        e.wait_ge(sem[ptg[0]], ptg[1])
                    e.activation(ptb, ppS[:, :], AF.Exp, bias=0.0,
                                 scale=ISQ).then_inc(sem["act"], 1)
                em("scalar", fE)
                bank_guard[tagS] = ("act", av)
                dve0 = C["dve"]
                for u in range(4):
                    m = g * 4 + u - j * 16
                    if 0 <= m < 16:
                        C["dve"] += 1
                        def fM(e, av=av, ptb=ptb, u=u, j=j, m=m):
                            e.wait_ge(sem["act"], av)
                            e.tensor_mul(ptb[:, u * 256:(u + 1) * 256],
                                         ptb[:, u * 256:(u + 1) * 256],
                                         mask_view(j, m)).then_inc(sem["dve"], 1)
                        em("vector", fM)
                masked = C["dve"] > dve0
                gp45 = bank_guard.get("p45")
                C["pe"] += 1
                pav = C["pe"]
                def fA(e, g=g, ptb=ptb, hb=hb, av=av, dv=C["dve"], masked=masked,
                       first=(g == 0), last=(g == ngrp - 1), gp45=gp45):
                    if masked:
                        e.wait_ge(sem["dve"], dv)
                    else:
                        e.wait_ge(sem["act"], av)
                    if first and gp45 is not None:
                        e.wait_ge(sem[gp45[0]], gp45[1])
                    for u in range(4):
                        kb = g * 4 + u
                        e.matmul(p4[:, 0:256], vh[hb][:, kb * 128:(kb + 1) * 128],
                                 ptb[:, u * 256:(u + 1) * 256],
                                 start=(first and u == 0), stop=(last and u == 3))
                    for u in range(4):
                        kb = g * 4 + u
                        ins = e.matmul(p5[:, 0:256], ones_b_sb,
                                       ptb[:, u * 256:(u + 1) * 256],
                                       start=(first and u == 0), stop=(last and u == 3))
                    ins.then_inc(sem["pe"], 1)
                em("tensor", fA)
                pt_guard[sb_i] = ("pe", pav)
                g_idx += 1
            ab = attb[(h * NSLOT + j) % 4]
            ag = att_guard[(h * NSLOT + j) % 4]
            C["dve"] += 2
            dv = C["dve"]
            def fT(e, pv=C["pe"], ab=ab, ag=ag, dv=dv):
                e.wait_ge(sem["pe"], pv)
                if ag is not None:
                    e.wait_ge(sem["st"], ag)
                e.reciprocal(rc_sb, p5[:, 0:256]).then_inc(sem["dve"], 1)
                e.wait_ge(sem["dve"], dv - 1)
                e.tensor_mul(ab, p4[:, 0:256], rc_sb).then_inc(sem["dve"], 1)
            em("vector", fT)
            bank_guard["p45"] = ("dve", dv)
            att_guard[(h * NSLOT + j) % 4] = dma_st(
                att_d[h * 128:(h + 1) * 128, j * 256:(j + 1) * 256], ab,
                pre=[("dve", dv)])
        kv_guard[hb] = C["pe"]
    p3_pe_done = C["pe"]
    st_att_done = C["st"]
    phase_mark("p3")

    RA, RB, RC, RD = BASE, BASE + 32768, BASE + 98304, BASE + 163840
    att_t = view(RA, BF16, DC * 512)
    x1b = view(RA, BF16, DC * 512)
    woslab = view(RB, BF16, DC * 1024)
    lat_sb = view(RB, BF16, 16 * 512)
    relu_t = [view(RB + 16384 + i * 2048, F32, 512) for i in range(2)]
    moe_sb = view(RB + 16384, BF16, 16 * 512)
    upslab = [view(RB + 32768 + i * 16384, BF16, DC * 256) for i in range(2)]
    dwslab = [view(RB + 32768, BF16, 64 * 128), view(RB + 49152, BF16, 64 * 128)]
    r1 = view(RC, F32, DC * 512)
    h_sb = view(RC, BF16, 64 * 512)
    r2 = view(RC, F32, DC * 512)
    sq = [view(RD + i * 2048, F32, 512) for i in range(2)]
    mu_sb = view(RD + 4096, F32, 512)
    rstd_sb = view(RD + 6144, F32, 512)
    var_sb = view(RD + 8192, F32, 512)
    xqtm = ([view(RD + 10240 + i * 512, F32, 128) for i in range(4)] +
            [view(174080 + i * 512, F32, 128) for i in range(4)])
    lg_sb = view(RD + 4096, F32, 4 * NE)
    t2_sb = view(RD + 4096 + 256, F32, 4)
    scr0 = view(RD + 4096 + 320, F32, 8)
    scr1 = view(RD + 4096 + 384, F32, 8)
    Rr = view(RD + 6144, F32, 512)
    t2b = view(RD + 8192, F32, 512)
    mu2_sb = view(RD + 6144, F32, 512)
    rstd2_sb = view(RD + 10240, F32, 512)
    mu3_sb = view(RD + 6144, F32, 512)
    rstd3_sb = view(RD + 10240, F32, 512)
    Sring = [view(RA + i * 2048, F32, 512) for i in range(4)]
    f2slab = [view(RA + 8192 + i * 8192, BF16, 16 * 256) for i in range(2)]
    Sld = [view(RA + 16384 + i * 2048, F32, 512) for i in range(2)]
    x1ld = [view(RA + 20480 + i * 2048, F32, 512) for i in range(2)]
    osb = view(RA + 8192, F32, D)
    Sld = [view(RA + 24576 + i * 2048, F32, 512) for i in range(2)]
    x1ld = [view(RA + 28672 + i * 2048, F32, 512) for i in range(2)]

    def fin_stats(mu_dst, rstd_dst, wait_pe):
        base = C["dve"]
        C["dve"] += 4
        dv1 = C["dve"]
        def f1(e, wait_pe=wait_pe, mu_dst=mu_dst, rstd_dst=rstd_dst, base=base):
            e.wait_ge(sem["pe"], wait_pe)
            e.tensor_scalar(mu_dst, p6, 1.0 / D, None, OP.mult).then_inc(sem["dve"], 1)
            e.tensor_scalar(var_sb, p7, 1.0 / D, None, OP.mult).then_inc(sem["dve"], 1)
            e.wait_ge(sem["dve"], base + 2)
            e.tensor_mul(rstd_dst, mu_dst, mu_dst).then_inc(sem["dve"], 1)
            e.wait_ge(sem["dve"], base + 3)
            e.tensor_sub(var_sb, var_sb, rstd_dst).then_inc(sem["dve"], 1)
        em("vector", f1)
        C["act"] += 1
        av = C["act"]
        def f2(e, dv1=dv1, rstd_dst=rstd_dst):
            e.wait_ge(sem["dve"], dv1)
            e.activation(rstd_dst, var_sb, AF.Sqrt, bias=eps_sb,
                         scale=1.0).then_inc(sem["act"], 1)
        em("scalar", f2)
        C["dve"] += 1
        dv2 = C["dve"]
        def f3(e, av=av, rstd_dst=rstd_dst):
            e.wait_ge(sem["act"], av)
            e.reciprocal(rstd_dst, rstd_dst).then_inc(sem["dve"], 1)
        em("vector", f3)
        bank_guard["p6"] = ("dve", dv1)
        bank_guard["p7"] = ("dve", dv1)
        return dv2

    s_guard = [None] * 4
    st_x1_done = {}
    st_s_done = {}
    slab_guard = {}
    ln2_dve_guard = {}
    os_guard = [None]
    p45_free_for_p4 = ("dve", C["dve"])
    for tile in range(2):
        slab_guard.clear()
        t0 = tile * 512
        atv = []
        att_pre = [("st", st_att_done), ("pe", p3_pe_done)] if tile == 0 else \
                  [("pe", C["pe"])]
        for ac in range(DC):
            atv.append(dma_ld(att_t[:, ac * 512:(ac + 1) * 512],
                              att_d[ac * 128:(ac + 1) * 128, t0:t0 + 512],
                              pre=att_pre if ac == 0 else None))
        att_ready = atv[-1]
        woslab_pe_done = 0
        for jb in range(4):
            pre = [("cst", cst_wo)] if (tile == 0 and jb == 0) else []
            pre.append(("pe", woslab_pe_done if woslab_pe_done else C["pe"]))
            sl = []
            for ac in range(DC):
                sl.append(dma_ldx(
                    woslab[:, ac * 1024:(ac + 1) * 1024],
                    wo16[jb * 1024:(jb + 1) * 1024, ac * 128:(ac + 1) * 128],
                    pre=pre if ac == 0 else None))
            slab_ready = sl[-1]
            for jc8 in range(8):
                jg = jb * 8 + jc8
                tag = "p4" if (jc8 % 2 == 0) else "p5"
                pp = p4 if (jc8 % 2 == 0) else p5
                xv = []
                xbase = (jg % 2) * 4
                for t1 in range(4):
                    xv.append(dma_ld(
                        xqtm[xbase + t1],
                        xq[t0 + t1 * 128: t0 + (t1 + 1) * 128, jg * 128:(jg + 1) * 128],
                        pre=[("pe", C["pe"])] if t1 == 0 else None))
                gd = bank_guard.get(tag) if (tile == 0 and jb == 0 and jc8 < 2) \
                    else bank_guard.get(tag)
                if tile == 0 and jb == 0 and jc8 < 2:
                    gd = p45_free_for_p4
                C["pe"] += 1
                pv = C["pe"]
                def fO(e, jc8=jc8, pp=pp, att_ready=att_ready, gd=gd,
                       slab_ready=slab_ready, xv=tuple(xv), xb=(jg % 2) * 4):
                    e.wait_ge(sem["ld"], max(att_ready, slab_ready))
                    if gd is not None:
                        e.wait_ge(sem[gd[0]], gd[1])
                    for ac in range(DC):
                        e.matmul(pp[:, :],
                                 woslab[:, ac * 1024 + jc8 * 128: ac * 1024 + jc8 * 128 + 128],
                                 att_t[:, ac * 512:(ac + 1) * 512],
                                 start=(ac == 0), stop=False)
                    for t1 in range(4):
                        e.wait_ge(sem["ld"], xv[t1])
                        ins = e.matmul(pp[:, t1 * 128:(t1 + 1) * 128],
                                       xqtm[xb + t1], ident_sb,
                                       is_transpose=True, start=False,
                                       stop=(t1 == 3))
                    ins.then_inc(sem["pe"], 1)
                em("tensor", fO)
                C["act"] += 1
                av = C["act"]
                def gO(e, pv=pv, pp=pp, jg=jg):
                    e.wait_ge(sem["pe"], pv)
                    e.activation(r1[:, jg * 512:(jg + 1) * 512], pp[:, :],
                                 AF.Identity, bias=bo_sb[:, jg:jg + 1],
                                 scale=1.0).then_inc(sem["act"], 1)
                em("scalar", gO)
                bank_guard[tag] = ("act", av)
            woslab_pe_done = C["pe"]
        act_op_done = C["act"]

        gp6 = bank_guard.get("p6") or bank_guard.get("pC")
        gp7 = bank_guard.get("p7") or bank_guard.get("pC")
        if tile == 0:
            gp6 = bank_guard.get("pC") or gp6
            gp7 = bank_guard.get("pC") or gp7
        for jg in range(DC):
            C["act"] += 1
            av = C["act"]
            def gSq(e, jg=jg, wa=act_op_done, buf=sq[jg % 2],
                    wpe=(C["pe"] if jg < 2 else C["pe"] - 1)):
                e.wait_ge(sem["act"], wa)
                e.wait_ge(sem["pe"], wpe)
                e.activation(buf, r1[:, jg * 512:(jg + 1) * 512],
                             AF.Square).then_inc(sem["act"], 1)
            em("scalar", gSq)
            C["pe"] += 1
            def fSt(e, jg=jg, av=av, buf=sq[jg % 2], gp6=gp6, gp7=gp7,
                    first=(jg == 0), last=(jg == DC - 1)):
                e.wait_ge(sem["act"], av)
                if first and gp6 is not None:
                    e.wait_ge(sem[gp6[0]], gp6[1])
                if first and gp7 is not None:
                    e.wait_ge(sem[gp7[0]], gp7[1])
                e.matmul(p6, ones_f_sb, r1[:, jg * 512:(jg + 1) * 512],
                         start=first, stop=last)
                ins = e.matmul(p7, ones_f_sb, buf, start=first, stop=last)
                ins.then_inc(sem["pe"], 1)
            em("tensor", fSt)
        dv_st1 = fin_stats(mu_sb, rstd_sb, C["pe"])

        for jg in range(DC):
            C["dve"] += 2
            dv = C["dve"]
            def fAp(e, jg=jg, dvs=dv_st1, dv=dv):
                e.wait_ge(sem["dve"], dvs)
                dst = r1[:, jg * 512:(jg + 1) * 512]
                e.tensor_sub(dst, dst, mu_sb).then_inc(sem["dve"], 1)
                e.wait_ge(sem["dve"], dv - 1)
                e.tensor_mul(dst, dst, rstd_sb).then_inc(sem["dve"], 1)
            em("vector", fAp)
            C["act"] += 1
            av = C["act"]
            def fCa(e, jg=jg, dv=dv):
                e.wait_ge(sem["dve"], dv)
                e.activation(x1b[:, jg * 512:(jg + 1) * 512],
                             r1[:, jg * 512:(jg + 1) * 512],
                             AF.Copy).then_inc(sem["act"], 1)
            em("scalar", fCa)
            dma_st(x1d[jg * 128:(jg + 1) * 128, t0:t0 + 512],
                   r1[:, jg * 512:(jg + 1) * 512], pre=[("act", av)])
        st_x1_done[tile] = C["st"]
        act_x1b_done = C["act"]
        phase_mark(f"ln1_{tile}")

        for t1 in range(4):
            gd = bank_guard.get("p6")
            C["pe"] += 1
            pv = C["pe"]
            def fG(e, t1=t1, wa=act_x1b_done, gd=gd, cg=cst_gate):
                e.wait_ge(sem["cst"], cg)
                e.wait_ge(sem["act"], wa)
                if gd is not None:
                    e.wait_ge(sem[gd[0]], gd[1])
                for dc in range(DC):
                    ins = e.matmul(
                        pC[:, 0:NE],
                        x1b[:, dc * 512 + t1 * 128: dc * 512 + t1 * 128 + 128],
                        gate_sb[:, dc * NE:(dc + 1) * NE],
                        start=(dc == 0), stop=(dc == DC - 1))
                ins.then_inc(sem["pe"], 1)
            em("tensor", fG)
            base = C["dve"]
            C["dve"] += 22
            dvt = C["dve"]
            def fL(e, pv=pv, t1=t1, base=base):
                e.wait_ge(sem["pe"], pv)
                lg = lg_sb[:, t1 * NE:(t1 + 1) * NE]
                n = base
                e.tensor_copy(lg, pC[:, 0:NE]).then_inc(sem["dve"], 1); n += 1
                e.wait_ge(sem["dve"], n)
                first = True
                for d in range(1, NE):
                    w_ = NE - d
                    e.tensor_add(scr0[:, 0:w_], lg[:, 0:w_], lg[:, d:d + w_]).then_inc(sem["dve"], 1); n += 1
                    e.wait_ge(sem["dve"], n)
                    e.reduce_max(scr1[:, 0:1], scr0[:, 0:w_], AX).then_inc(sem["dve"], 1); n += 1
                    e.wait_ge(sem["dve"], n)
                    if first:
                        e.tensor_copy(t2_sb[:, t1:t1 + 1], scr1[:, 0:1]).then_inc(sem["dve"], 1); n += 1
                        first = False
                    else:
                        e.tensor_max(t2_sb[:, t1:t1 + 1], t2_sb[:, t1:t1 + 1],
                                     scr1[:, 0:1]).then_inc(sem["dve"], 1); n += 1
                    e.wait_ge(sem["dve"], n)
            em("vector", fL)
            bank_guard["p6"] = ("dve", dvt)
            gd7 = bank_guard.get("p7")
            C["pe"] += 1
            pvt = C["pe"]
            def fTr(e, dvt=dvt, t1=t1, gd7=gd7):
                e.wait_ge(sem["dve"], dvt)
                if t1 == 0 and gd7 is not None:
                    e.wait_ge(sem[gd7[0]], gd7[1])
                e.matmul(pC[0:1, 512:640], t2_sb[:, t1:t1 + 1], ident_sb,
                         is_transpose=True, start=True,
                         stop=True).then_inc(sem["pe"], 1)
            em("tensor", fTr)
            C["dve"] += (2 if t1 == 0 else 1)
            dvr = C["dve"] - 1 if t1 == 0 else C["dve"]
            def fRr(e, pvt=pvt, t1=t1, dv=dvr):
                e.wait_ge(sem["pe"], pvt)
                if t1 == 0:
                    e.memset(Rr, 0.0).then_inc(sem["dve"], 1)
                    e.wait_ge(sem["dve"], dv)
                e.tensor_copy(Rr[0:1, t1 * 128:(t1 + 1) * 128],
                              pC[0:1, 512:640]).then_inc(sem["dve"], 1)
            em("vector", fRr)
            bank_guard["p7"] = ("dve", C["dve"])
        C["pe"] += 1
        pvb = C["pe"]
        def fB(e, wd=C["dve"], gd=bank_guard.get("p7")):
            e.wait_ge(sem["dve"], wd)
            e.matmul(p7, ones_f_sb, Rr, start=True,
                     stop=True).then_inc(sem["pe"], 1)
        em("tensor", fB)
        C["dve"] += 1
        dv_t2b = C["dve"]
        def fB2(e, pvb=pvb):
            e.wait_ge(sem["pe"], pvb)
            e.tensor_copy(t2b, p7).then_inc(sem["dve"], 1)
        em("vector", fB2)
        bank_guard["p7"] = ("dve", dv_t2b)

        for mp in range(32):
            mb = mp % 2
            g = slab_guard.get(("up", mb)) if mp >= 2 else C["pe"]
            upr = dma_ld(
                upslab[mb].rearrange("p (kc n) -> p kc n", n=256),
                up16.ap().rearrange("(kc p) n -> p kc n", p=128)[:, :, mp * 256:(mp + 1) * 256],
                pre=[("pe", g)] + ([("cst", cst_ffn)] if (tile == 0 and mp == 0) else []))
            for s in range(2):
                m = 2 * mp + s
                tag = "p4" if m % 2 == 0 else "p5"
                pp = p4 if m % 2 == 0 else p5
                gd = bank_guard.get(tag)
                C["pe"] += 1
                pv = C["pe"]
                def fU(e, mb=mb, s=s, pp=pp, upr=upr, gd=gd, m=m,
                       wst=st_x1_done[tile]):
                    e.wait_ge(sem["ld"], upr)
                    if m == 0:
                        e.wait_ge(sem["st"], wst)
                    if gd is not None:
                        e.wait_ge(sem[gd[0]], gd[1])
                    for dc in range(DC):
                        ins = e.matmul(pp[:, :],
                                       upslab[mb][:, dc * 256 + s * 128: dc * 256 + s * 128 + 128],
                                       x1b[:, dc * 512:(dc + 1) * 512],
                                       start=(dc == 0), stop=(dc == DC - 1))
                    ins.then_inc(sem["pe"], 1)
                em("tensor", fU)
                slab_guard[("up", mb)] = pv
                C["act"] += 1
                av = C["act"]
                def fR(e, pv=pv, pp=pp, mb2=m % 2, wd=C["dve"]):
                    e.wait_ge(sem["pe"], pv)
                    e.wait_ge(sem["dve"], wd)
                    e.activation(relu_t[mb2], pp[:, :], AF.Relu).then_inc(sem["act"], 1)
                em("scalar", fR)
                bank_guard[tag] = ("act", av)
                C["dve"] += 1
                def fH(e, av=av, mb2=m % 2, m=m):
                    e.wait_ge(sem["act"], av)
                    e.tensor_mul(h_sb[:, m * 512:(m + 1) * 512], relu_t[mb2],
                                 relu_t[mb2]).then_inc(sem["dve"], 1)
                em("vector", fH)
        for mp in range(8):
            mb = mp % 2
            g = slab_guard[("up", mb)]
            fr = dma_ld(
                upslab[mb].rearrange("p (kc n) -> p kc n", n=256),
                fc1_16.ap().rearrange("(kc p) n -> p kc n", p=128)[:, :, mp * 256:(mp + 1) * 256],
                pre=[("pe", g)])
            for s in range(2):
                m = 2 * mp + s
                tag = "p4" if m % 2 == 0 else "p5"
                pp = p4 if m % 2 == 0 else p5
                gd = bank_guard.get(tag)
                C["pe"] += 1
                pv = C["pe"]
                def fF1(e, mb=mb, s=s, pp=pp, fr=fr, gd=gd):
                    e.wait_ge(sem["ld"], fr)
                    if gd is not None:
                        e.wait_ge(sem[gd[0]], gd[1])
                    for dc in range(DC):
                        ins = e.matmul(pp[:, :],
                                       upslab[mb][:, dc * 256 + s * 128: dc * 256 + s * 128 + 128],
                                       x1b[:, dc * 512:(dc + 1) * 512],
                                       start=(dc == 0), stop=(dc == DC - 1))
                    ins.then_inc(sem["pe"], 1)
                em("tensor", fF1)
                slab_guard[("up", mb)] = pv
                C["act"] += 1
                av = C["act"]
                def fLa(e, pv=pv, pp=pp, m=m):
                    e.wait_ge(sem["pe"], pv)
                    e.activation(lat_sb[:, m * 512:(m + 1) * 512], pp[:, :],
                                 AF.Copy).then_inc(sem["act"], 1)
                em("scalar", fLa)
                bank_guard[tag] = ("act", av)
        act_lat_done = C["act"]

        for mp in range(8):
            mb = mp % 2
            g = slab_guard.get(("f2", mb)) if mp >= 2 else C["pe"]
            er = dma_ld(
                f2slab[mb].rearrange("p (kc n) -> p kc n", n=256),
                exp16.ap().rearrange("(kc p) n -> p kc n", p=128)[:, :, mp * 256:(mp + 1) * 256],
                pre=[("pe", g)])
            for s in range(2):
                m = 2 * mp + s
                tag = "p4" if m % 2 == 0 else "p5"
                pp = p4 if m % 2 == 0 else p5
                gd = bank_guard.get(tag)
                C["pe"] += 1
                pv = C["pe"]
                def fEx(e, mb=mb, s=s, pp=pp, er=er, gd=gd, wa=act_lat_done,
                        wd=dv_t2b):
                    e.wait_ge(sem["ld"], er)
                    e.wait_ge(sem["act"], wa)
                    e.wait_ge(sem["dve"], wd)
                    if gd is not None:
                        e.wait_ge(sem[gd[0]], gd[1])
                    for lc in range(16):
                        ins = e.matmul(pp[:, :],
                                       f2slab[mb][:, lc * 256 + s * 128: lc * 256 + s * 128 + 128],
                                       lat_sb[:, lc * 512:(lc + 1) * 512],
                                       start=(lc == 0), stop=(lc == 15))
                    ins.then_inc(sem["pe"], 1)
                em("tensor", fEx)
                slab_guard[("f2", mb)] = pv
                C["dve"] += 1
                dv = C["dve"]
                def fMo(e, pv=pv, pp=pp, m=m):
                    e.wait_ge(sem["pe"], pv)
                    e.tensor_mul(moe_sb[:, m * 512:(m + 1) * 512], pp[:, :],
                                 t2b).then_inc(sem["dve"], 1)
                em("vector", fMo)
                bank_guard[tag] = ("dve", dv)
        dv_moe_done = C["dve"]
        phase_mark(f"moe_{tile}")

        gp6, gp7 = bank_guard.get("p6"), bank_guard.get("p7")
        for jg in range(DC):
            mb = jg % 2
            gdw = slab_guard.get(("dw", mb)) if jg >= 2 else C["pe"]
            if gdw is None:
                gdw = C["pe"]
            dr = dma_ld(
                dwslab[mb].rearrange("p (kc n) -> p kc n", n=128),
                down16.ap().rearrange("(kc p) n -> p kc n", p=128)[:, :, jg * 128:(jg + 1) * 128],
                pre=[("pe", gdw), ("dve", dv_moe_done)] if jg < 2 else
                    [("pe", gdw)])
            if jg % 2 == 0:
                fb2 = (jg // 2) % 2
                gf2 = slab_guard.get(("f2", fb2)) if jg >= 4 else C["pe"]
                fr_pair = dma_ld(
                    f2slab[fb2].rearrange("p (kc n) -> p kc n", n=256),
                    fc2_16.ap().rearrange("(kc p) n -> p kc n", p=128)[:, :, (jg // 2) * 256:(jg // 2) * 256 + 256],
                    pre=[("pe", gf2)])
            fr = fr_pair
            tag = "p4" if mb == 0 else "p5"
            pp = p4 if mb == 0 else p5
            gd = bank_guard.get(tag)
            C["pe"] += 1
            pv = C["pe"]
            fb2 = (jg // 2) % 2
            fs = jg % 2
            def fD(e, mb=mb, pp=pp, dr=dr, fr=fr, gd=gd, wd=dv_moe_done,
                   fb2=fb2, fs=fs):
                e.wait_ge(sem["ld"], max(dr, fr))
                e.wait_ge(sem["dve"], wd)
                if gd is not None:
                    e.wait_ge(sem[gd[0]], gd[1])
                for ic in range(64):
                    e.matmul(pp[:, :], dwslab[mb][:, ic * 128:(ic + 1) * 128],
                             h_sb[:, ic * 512:(ic + 1) * 512],
                             start=(ic == 0), stop=False)
                for lc in range(16):
                    ins = e.matmul(pp[:, :],
                                   f2slab[fb2][:, lc * 256 + fs * 128: lc * 256 + fs * 128 + 128],
                                   moe_sb[:, lc * 512:(lc + 1) * 512],
                                   start=False, stop=(lc == 15))
                ins.then_inc(sem["pe"], 1)
            em("tensor", fD)
            slab_guard[("dw", mb)] = pv
            slab_guard[("f2", fb2)] = pv
            buf, guard = Sring[jg % 4], s_guard[jg % 4]
            C["act"] += 1
            av = C["act"]
            def gS(e, pv=pv, pp=pp, buf=buf, guard=guard):
                e.wait_ge(sem["pe"], pv)
                if guard is not None:
                    e.wait_ge(sem["st"], guard)
                e.activation(buf, pp[:, :], AF.Copy).then_inc(sem["act"], 1)
            em("scalar", gS)
            bank_guard[tag] = ("act", av)
            s_guard[jg % 4] = dma_st(sd[jg * 128:(jg + 1) * 128, t0:t0 + 512],
                                     buf, pre=[("act", av)])
            C["act"] += 1
            av2 = C["act"]
            def gSq2(e, av=av, buf=buf, sqb=sq[jg % 2],
                     wpe=(C["pe"] if jg < 2 else C["pe"] - 2)):
                e.wait_ge(sem["act"], av)
                e.wait_ge(sem["pe"], wpe)
                e.activation(sqb, buf, AF.Square).then_inc(sem["act"], 1)
            em("scalar", gSq2)
            C["pe"] += 1
            def fSt2(e, av2=av2, buf=buf, sqb=sq[jg % 2], gp6=gp6, gp7=gp7,
                     first=(jg == 0), last=(jg == DC - 1)):
                e.wait_ge(sem["act"], av2)
                if first and gp6 is not None:
                    e.wait_ge(sem[gp6[0]], gp6[1])
                if first and gp7 is not None:
                    e.wait_ge(sem[gp7[0]], gp7[1])
                e.matmul(p6, ones_f_sb, buf, start=first, stop=last)
                ins = e.matmul(p7, ones_f_sb, sqb, start=first, stop=last)
                ins.then_inc(sem["pe"], 1)
            em("tensor", fSt2)
        dv_st2 = fin_stats(mu2_sb, rstd2_sb, C["pe"])
        st_s_done[tile] = C["st"]
        phase_mark(f"down_{tile}")

        gp6, gp7 = bank_guard.get("p6"), bank_guard.get("p7")
        for jg in range(DC):
            gld = ln2_dve_guard.get(jg % 2)
            lS = dma_ld(Sld[jg % 2], sd[jg * 128:(jg + 1) * 128, t0:t0 + 512],
                        pre=([("st", st_s_done[tile])] if jg < 2 else []) +
                            ([("dve", gld)] if gld else []))
            lx = dma_ld(x1ld[jg % 2],
                        x1d[jg * 128:(jg + 1) * 128, t0:t0 + 512],
                        pre=[("st", st_x1_done[tile])] if jg < 2 else None)
            C["dve"] += 4
            dv = C["dve"]
            def fR2(e, jg=jg, lS=lS, lx=lx, dvs=dv_st2, sb=Sld[jg % 2],
                    xb=x1ld[jg % 2], base=dv - 4):
                e.wait_ge(sem["ld"], max(lS, lx))
                e.wait_ge(sem["dve"], dvs)
                dst = r2[:, jg * 512:(jg + 1) * 512]
                e.tensor_sub(dst, sb, mu2_sb).then_inc(sem["dve"], 1)
                e.wait_ge(sem["dve"], base + 1)
                e.tensor_mul(dst, dst, rstd2_sb).then_inc(sem["dve"], 1)
                e.wait_ge(sem["dve"], base + 2)
                e.tensor_scalar(dst, dst, g_sb[:, jg:jg + 1], b_sb[:, jg:jg + 1],
                                OP.mult, OP.add).then_inc(sem["dve"], 1)
                e.wait_ge(sem["dve"], base + 3)
                e.tensor_add(dst, dst, xb).then_inc(sem["dve"], 1)
            em("vector", fR2)
            ln2_dve_guard[jg % 2] = dv
            C["act"] += 1
            av3 = C["act"]
            def gSq3(e, dv=dv, jg=jg, sqb=sq[jg % 2],
                     wpe=(C["pe"] if jg < 2 else C["pe"] - 1)):
                e.wait_ge(sem["dve"], dv)
                e.wait_ge(sem["pe"], wpe)
                e.activation(sqb, r2[:, jg * 512:(jg + 1) * 512],
                             AF.Square).then_inc(sem["act"], 1)
            em("scalar", gSq3)
            C["pe"] += 1
            def fSt3(e, av3=av3, jg=jg, sqb=sq[jg % 2], gp6=gp6, gp7=gp7,
                     first=(jg == 0), last=(jg == DC - 1)):
                e.wait_ge(sem["act"], av3)
                if first and gp6 is not None:
                    e.wait_ge(sem[gp6[0]], gp6[1])
                if first and gp7 is not None:
                    e.wait_ge(sem[gp7[0]], gp7[1])
                e.matmul(p6, ones_f_sb, r2[:, jg * 512:(jg + 1) * 512],
                         start=first, stop=last)
                ins = e.matmul(p7, ones_f_sb, sqb, start=first, stop=last)
                ins.then_inc(sem["pe"], 1)
            em("tensor", fSt3)
        dv_st3 = fin_stats(mu3_sb, rstd3_sb, C["pe"])

        for jg in range(DC):
            C["dve"] += 2
            def fX2(e, jg=jg, dvs=dv_st3, dv=C["dve"]):
                e.wait_ge(sem["dve"], dvs)
                dst = r2[:, jg * 512:(jg + 1) * 512]
                e.tensor_sub(dst, dst, mu3_sb).then_inc(sem["dve"], 1)
                e.wait_ge(sem["dve"], dv - 1)
                e.tensor_mul(dst, dst, rstd3_sb).then_inc(sem["dve"], 1)
            em("vector", fX2)
        dv_x2_done = C["dve"]
        for t1 in range(4):
            for jg in range(DC):
                tag = "p6" if (jg % 2 == 0) else "p7"
                pp = p6 if (jg % 2 == 0) else p7
                gd = bank_guard.get(tag)
                C["pe"] += 1
                pv = C["pe"]
                def fTr2(e, jg=jg, t1=t1, pp=pp, gd=gd, wd=dv_x2_done):
                    e.wait_ge(sem["dve"], wd)
                    if gd is not None:
                        e.wait_ge(sem[gd[0]], gd[1])
                    e.matmul(pp[:, 0:128],
                             r2[:, jg * 512 + t1 * 128: jg * 512 + t1 * 128 + 128],
                             ident_sb, is_transpose=True, start=True,
                             stop=True).then_inc(sem["pe"], 1)
                em("tensor", fTr2)
                C["act"] += 1
                av = C["act"]
                def gE(e, pv=pv, jg=jg, pp=pp, og=os_guard[0] if jg == 0 else None):
                    e.wait_ge(sem["pe"], pv)
                    if og is not None:
                        e.wait_ge(sem["st"], og)
                    e.activation(osb[:, jg * 128:(jg + 1) * 128], pp[:, 0:128],
                                 AF.Copy).then_inc(sem["act"], 1)
                em("scalar", gE)
                bank_guard[tag] = ("act", av)
            os_guard[0] = dma_st(out_rows[t0 + t1 * 128: t0 + (t1 + 1) * 128, :],
                                 osb, pre=[("act", C["act"])])

    phase_mark("all")

    if stop_after is not None:
        lens, cc = marks[stop_after]
        for e in ENGS:
            Q[e] = Q[e][:lens[e]]
        em("sync", lambda e, v=cc["ld"]: e.wait_ge(sem["ld"], v))
        em("scalar", lambda e, v=cc["act"]: e.wait_ge(sem["act"], v))
        em("vector", lambda e, v=cc["dve"]: e.wait_ge(sem["dve"], v))
        em("tensor", lambda e, v=cc["pe"]: e.wait_ge(sem["pe"], v))
        C["st"] = cc["st"]
        if debug_dump in ("hsb", "latsb", "moesb", "x1bsb", "t2bsb"):
            bufmap = {"hsb": (h_sb, 32768), "latsb": (lat_sb, 8192),
                      "moesb": (moe_sb, 8192), "x1bsb": (x1b, 16384),
                      "t2bsb": (t2b, 512)}
            bb, ncols = bufmap[debug_dump]
            ov = out_rows.ap().rearrange("(b p) d -> p b d", p=128)
            def fdump2(e, v=cc["cst"], w=cc["st"], bb=bb, ncols=ncols, ov=ov):
                e.wait_ge(sem["cst"], v)
                e.wait_ge(sem["st"], w)
                e.wait_ge(sem["dve"], cc["dve"])
                e.wait_ge(sem["act"], cc["act"])
                nb = (ncols + 4095) // 4096
                e.dma_start(out=ov[:, 0:nb, 0:min(ncols, 4096)],
                            in_=bb[:, 0:ncols].rearrange("p (b d) -> p b d", b=nb)
                            if ncols > 4096 else bb[:, 0:ncols].rearrange("p d -> p 1 d" if False else "p (b d) -> p b d", b=1)
                            ).then_inc(sem["cst"], 16)
                e.wait_ge(sem["cst"], v + 16)
            em("gpsimd", fdump2)
            debug_dump = None
        if debug_dump is not None:
            src_t = {"kd": kd_all, "vd": vd_all, "att": att_d, "x1": x1d,
                     "sdump": sd, "x16": xkv16, "q": None}[debug_dump]
            def fdump(e, v=cc["cst"], w=cc["st"]):
                e.wait_ge(sem["cst"], v)
                e.wait_ge(sem["st"], w)
                if debug_dump == "kd":
                    e.dma_start(out=out_rows[0:1024, 0:1024], in_=kd_all[0:1024, 0:1024]).then_inc(sem["cst"], 16)
                elif debug_dump == "vd":
                    e.dma_start(out=out_rows[:, :], in_=vd_all[0:1024, 0:4096]).then_inc(sem["cst"], 16)
                elif debug_dump == "x16":
                    e.dma_start(out=out_rows[0:1024, :], in_=xkv16[0:1024, 0:4096]).then_inc(sem["cst"], 16)
                elif debug_dump == "att":
                    e.dma_start(out=out_rows[0:1024, 0:1024], in_=att_d[0:1024, :]).then_inc(sem["cst"], 16)
                elif debug_dump == "x1":
                    e.dma_start(out=out_rows[0:1024, 0:1024], in_=x1d[0:1024, :]).then_inc(sem["cst"], 16)
                elif debug_dump == "sdump":
                    e.dma_start(out=out_rows[0:1024, 0:1024], in_=sd[0:1024, :]).then_inc(sem["cst"], 16)
                e.wait_ge(sem["cst"], v + 16)
            em("gpsimd", fdump)
        else:
            em("gpsimd", lambda e, v=cc["cst"]: e.wait_ge(sem["cst"], v))

    em("scalar", lambda e, v=C["st"]: e.wait_ge(sem["st"], v))

    @block.sync
    def _(e):
        for f in Q["sync"]:
            f(e)

    @block.gpsimd
    def _(e):
        for f in Q["gpsimd"]:
            f(e)

    @block.scalar
    def _(e):
        for f in Q["scalar"]:
            f(e)

    @block.vector
    def _(e):
        for f in Q["vector"]:
            f(e)

    @block.tensor
    def _(e):
        for f in Q["tensor"]:
            f(e)

    ctx.close()
    return nc


def _host_inputs(inputs):
    x = np.ascontiguousarray(np.asarray(inputs["L0_attn_x"], dtype=np.float32)[0])
    fm = lambda v: np.ascontiguousarray(
        np.asarray(v, dtype=np.float32).reshape(DC, 128).T)
    base = {
        "wq": np.ascontiguousarray(np.asarray(inputs["L0_attn_Wq"], np.float32)),
        "wk": np.ascontiguousarray(np.asarray(inputs["L0_attn_Wk"], np.float32)),
        "wv": np.ascontiguousarray(np.asarray(inputs["L0_attn_Wv"], np.float32)),
        "wo": np.ascontiguousarray(np.asarray(inputs["L0_attn_Wo"], np.float32)),
        "bq_fm": fm(inputs["L0_attn_bq"]),
        "bk_fm": fm(inputs["L0_attn_bk"]),
        "bo_fm": fm(inputs["L0_attn_bo"]),
        "g_fm": fm(inputs["L0_ffn_ln_gamma"]),
        "b_fm": fm(inputs["L0_ffn_ln_beta"]),
        "bv_row": np.ascontiguousarray(
            np.asarray(inputs["L0_attn_bv"], np.float32).reshape(1, D)),
        "gate_w": np.ascontiguousarray(np.asarray(inputs["L0_ffn_gate_w"], np.float32)),
        "up_w": np.ascontiguousarray(np.asarray(inputs["L0_ffn_up_w"], np.float32)),
        "down_w": np.ascontiguousarray(np.asarray(inputs["L0_ffn_down_w"], np.float32)),
        "fc1_w": np.ascontiguousarray(np.asarray(inputs["L0_ffn_fc1_w"], np.float32)),
        "fc2_w": np.ascontiguousarray(np.asarray(inputs["L0_ffn_fc2_w"], np.float32)),
        "exp_w": np.ascontiguousarray(np.asarray(inputs["L0_ffn_expert_w"], np.float32)),
    }
    in_maps, row_idx = [], []
    for c in range(NC):
        rows = np.concatenate([np.arange((8 * j + c) * QW, (8 * j + c + 1) * QW)
                               for j in range(NSLOT)])
        row_idx.append(rows)
        mk = np.zeros((NSLOT, 16, 128, QW), dtype=np.float32)
        for j in range(NSLOT):
            qpos = (8 * j + c) * QW + np.arange(QW)[None, :]
            for m in range(16):
                kpos = (16 * j + m) * 128 + np.arange(128)[:, None]
                mk[j, m] = (kpos <= qpos)
        m = dict(base)
        m["xq"] = np.ascontiguousarray(x[rows])
        m["xkv"] = np.ascontiguousarray(x[c * TQ:(c + 1) * TQ])
        m["masks"] = mk.astype(ml_dtypes.bfloat16)
        in_maps.append(m)
    return in_maps, row_idx


def kernel(**inputs):
    in_maps, row_idx = _host_inputs(inputs)
    nc = build_nc()
    res = run_bass_kernel_spmd(nc, in_maps, core_ids=list(range(NC)))
    out = np.empty((T, D), dtype=np.float32)
    for c in range(NC):
        out[row_idx[c]] = res.results[c]["out_rows"]
    return out.reshape(1, T, D)
